# version 14
# baseline (speedup 1.0000x reference)
import contextlib

import numpy as np
import ml_dtypes

import concourse.bass as bass
import concourse.mybir as mybir
from concourse.bass_utils import run_bass_kernel_spmd

F32 = mybir.dt.float32
BF16 = mybir.dt.bfloat16
I8 = mybir.dt.int8
AF = mybir.ActivationFunctionType
ALU = mybir.AluOpType
AX = mybir.AxisListType.X

D = 2048
H = 16
S = 4096
T = 1024
NT = 32
OT = 8
DC = 16
IC = 43
NITER = 3 * IC


def g_mat_ic(k):
    if k < 2 * IC:
        return (k % 2, k // 2)
    return (2, k - 2 * IC)


def build():
    nc = bass.Bass("TRN2", target_bir_lowering=False)

    xb = nc.dram_tensor("xb", [S, D], F32, kind="ExternalInput").ap()
    xo = nc.dram_tensor("xo", [T, D], F32, kind="ExternalInput").ap()
    wkT = nc.dram_tensor("wkT", [D, D], F32, kind="ExternalInput").ap()
    wvT = nc.dram_tensor("wvT", [D, D], F32, kind="ExternalInput").ap()
    wqT = nc.dram_tensor("wqT", [D, D], F32, kind="ExternalInput").ap()
    woT = nc.dram_tensor("woT", [D, D], F32, kind="ExternalInput").ap()
    bk = nc.dram_tensor("bk", [1, D], F32, kind="ExternalInput").ap()
    bv = nc.dram_tensor("bv", [1, D], F32, kind="ExternalInput").ap()
    bq = nc.dram_tensor("bq", [1, D], F32, kind="ExternalInput").ap()
    bo = nc.dram_tensor("bo", [1, D], F32, kind="ExternalInput").ap()
    g1 = nc.dram_tensor("g1", [IC, 128, D], I8, kind="ExternalInput").ap()
    g2 = nc.dram_tensor("g2", [IC, 128, D], I8, kind="ExternalInput").ap()
    d3 = nc.dram_tensor("d3", [IC, 128, D], I8, kind="ExternalInput").ap()
    lut16 = nc.dram_tensor("lut16", [1, 16], F32, kind="ExternalInput").ap()
    dk_cols = nc.dram_tensor("dk_cols", [128, NT], F32, kind="ExternalInput").ap()
    dq_cols = nc.dram_tensor("dq_cols", [128, OT], F32, kind="ExternalInput").ap()
    eye_in = nc.dram_tensor("eye_in", [128, 128], BF16, kind="ExternalInput").ap()
    y = nc.dram_tensor("y", [T, D], F32, kind="ExternalOutput").ap()
    wh = [nc.dram_tensor(f"w{m}h", [IC, 128, D], BF16, kind="Internal").ap() for m in range(3)]
    gsrc = [g1, g2, d3]

    C = {}

    def mk_sem(es, name):
        h = es.enter_context(nc.semaphore(name))
        C[name] = 0
        return h

    def inc(instr, name, h, n):
        C[name] += n
        instr.then_inc(h, n)
        return C[name]

    BB = [0]

    es0 = contextlib.ExitStack()
    with es0:
        ones_bf = es0.enter_context(nc.sbuf_tensor("ones_bf", [1, 1024], BF16))
        onesP_bf = es0.enter_context(nc.sbuf_tensor("onesP_bf", [1, 128], BF16))
        onesC_bf = es0.enter_context(nc.sbuf_tensor("onesC_bf", [128, 1], BF16))
        eye_sb = es0.enter_context(nc.sbuf_tensor("eye_sb", [128, 128], BF16))
        lut_s = es0.enter_context(nc.sbuf_tensor("lut_s", [1, 16], F32))
        lut_b = es0.enter_context(nc.sbuf_tensor("lut_b", [1, 16], BF16))
        lutb = es0.enter_context(nc.sbuf_tensor("lutb", [128, 16], F32))
        dk_sb = es0.enter_context(nc.sbuf_tensor("dk_sb", [128, NT], F32))
        dq_sb = es0.enter_context(nc.sbuf_tensor("dq_sb", [128, OT], F32))
        epsln = es0.enter_context(nc.sbuf_tensor("epsln", [128, 1], F32))
        epsat = es0.enter_context(nc.sbuf_tensor("epsat", [128, 1], F32))
        st = es0.enter_context(nc.sbuf_tensor("st", [128, 32], F32))
        gi = [es0.enter_context(nc.sbuf_tensor(f"gi{i}", [128, D], I8)) for i in range(2)]
        gbs = [es0.enter_context(nc.sbuf_tensor(f"gb{i}", [128, D], BF16)) for i in range(2)]
        wts = [es0.enter_context(nc.sbuf_tensor(f"wt{i}", [128, D], BF16)) for i in range(2)]
        planes = [es0.enter_context(nc.sbuf_tensor(f"plane{i}", [128, D], BF16)) for i in range(2)]
        wt2 = es0.enter_context(nc.sbuf_tensor("wt2", [128, D], BF16))

        KV_sb = es0.enter_context(nc.sbuf_tensor("KV_sb", [128, H * 128], BF16))
        Ksum_sb = es0.enter_context(nc.sbuf_tensor("Ksum_sb", [128, H], BF16))
        zt_sb = es0.enter_context(nc.sbuf_tensor("zt_sb", [128, H], F32))
        zr_sb = es0.enter_context(nc.sbuf_tensor("zr_sb", [128, H], F32))
        zde_sb = es0.enter_context(nc.sbuf_tensor("zde_sb", [128, H], F32))

        s_misc = mk_sem(es0, "s_misc")
        s_xp = [mk_sem(es0, "s_x0"), mk_sem(es0, "s_x1")]
        s_xo = mk_sem(es0, "s_xo")
        s_ip = [mk_sem(es0, "s_i0"), mk_sem(es0, "s_i1")]
        s_mm = mk_sem(es0, "s_mm")
        s_ve = mk_sem(es0, "s_ve")
        s_sc = mk_sem(es0, "s_sc")
        s_gp = mk_sem(es0, "s_gp")
        s_gc = mk_sem(es0, "s_gc")
        s_gv = mk_sem(es0, "s_gv")
        s_vv = mk_sem(es0, "s_vv")
        s_go = mk_sem(es0, "s_go")
        s_oa = [mk_sem(es0, "s_out0"), mk_sem(es0, "s_out1")]
        s_ph = mk_sem(es0, "s_ph")

        def bwait(e):
            if BB[0] > 0:
                e.wait_ge(s_ph, 5 * BB[0])

        def bdone(e):
            e.sem_inc(s_ph, 1)

        def g_dma_idx(eng, k):
            m, ic = g_mat_ic(k)
            if k >= 2:
                eng.wait_ge(s_gc, VEC_IDX[k - 2] + 1)
            d = eng.dma_start(out=gi[k % 2][:], in_=gsrc[m][ic, :, :])
            inc(d, f"s_i{k % 2}", s_ip[k % 2], 16)

        def g_cast(eng, k):
            vi = VEC_IDX[k]
            eng.wait_ge(s_ip[k % 2], 16 * (k // 2 + 1))
            if k >= 2:
                eng.wait_ge(s_gv, VEC_IDX[k - 2] + 1)
            i = eng.tensor_copy(gbs[k % 2][:], gi[k % 2][:])
            inc(i, "s_gc", s_gc, 1)

        def g_front(gpsimd, k):
            if k + 1 < NITER:
                g_dma_idx(gpsimd, k + 1)
            g_cast(gpsimd, k)
            if k > 0:
                g_out(gpsimd, k - 1)
            if k == NITER - 1:
                g_out(gpsimd, k)

        def g_vec(vector, k):
            if k in GP_OWN:
                return
            vi = VEC_IDX[k]
            gb = gbs[k % 2]
            wt = wts[k % 2]
            vector.wait_ge(s_gc, vi + 1)
            if k > 1:
                vector.wait_ge(s_go, 16 * (k - 1))
            vector.tensor_scalar(wt[:], gb[:], 0.0, lutb[:, 0:1], ALU.is_equal, ALU.mult)
            for v in range(1, 16):
                vector.tensor_scalar(planes[v % 2][:], gb[:], float(v), lutb[:, v : v + 1],
                                     ALU.is_equal, ALU.mult)
                vector.drain()
                vector.tensor_tensor(wt[:], wt[:], planes[v % 2][:], ALU.add)
            i = vector.drain()
            inc(i, "s_gv", s_gv, 1)

        def g_out(gpsimd, k):
            m, ic = g_mat_ic(k)
            gpsimd.wait_ge(s_gv, VEC_IDX[k] + 1)
            if k > 0:
                gpsimd.wait_ge(s_go, 16 * k)
            d = gpsimd.dma_start(out=wh[m][ic, :, :], in_=wts[k % 2][:])
            inc(d, "s_go", s_go, 16)

        GP_OWN = set()
        VEC_IDX = {}
        _vi = 0
        for _k in range(NITER):
            if _k not in GP_OWN:
                VEC_IDX[_k] = _vi
                _vi += 1
        NVEC = _vi

        GA = list(range(0, 38))
        GB1 = list(range(38, 46))
        GB2 = list(range(46, 58))
        GB3 = list(range(58, 72))
        GB5a = list(range(72, 90))
        GB4 = list(range(90, NITER))
        GB5b = []

        def g_slice(lst, i, n):
            return lst[len(lst) * i // n : len(lst) * (i + 1) // n]

        def ln_vec_sum(vector, x_t, pb):
            b = 16 * pb
            return vector.reduce_sum(st[:, b : b + 1], x_t[:], axis=AX)

        def ln_scalar_pre(scalar, x_t, pb):
            b = 16 * pb
            scalar.activation(wt2[:], x_t[:], AF.Square, accum_out=st[:, b + 3 : b + 4])
            scalar.mul(st[:, b + 1 : b + 2], st[:, b : b + 1], -1.0 / D)
            return scalar.drain()

        def ln_vec_stats(vector, pb):
            b = 16 * pb
            vector.tensor_tensor(st[:, b + 4 : b + 5], st[:, b : b + 1], st[:, b : b + 1], ALU.mult)
            vector.drain()
            return vector.scalar_tensor_tensor(st[:, b + 5 : b + 6], st[:, b + 3 : b + 4], float(D),
                                               st[:, b + 4 : b + 5], ALU.mult, ALU.subtract)

        def ln_scalar_mid(scalar, pb):
            b = 16 * pb
            return scalar.activation(st[:, b + 6 : b + 7], st[:, b + 5 : b + 6], AF.Sqrt,
                                     bias=epsln[:, 0:1], scale=1.0 / (float(D) * float(D)))

        def ln_vec_recip(vector, pb):
            b = 16 * pb
            return vector.reciprocal(st[:, b + 7 : b + 8], st[:, b + 6 : b + 7])

        def ln_scalar_post(scalar, pb):
            b = 16 * pb
            return scalar.copy(st[:, b + 8 : b + 9], st[:, b + 7 : b + 8])

        def ln_vec_xn(vector, x_t, xn_out, pb):
            b = 16 * pb
            return vector.tensor_scalar(xn_out[:], x_t[:], st[:, b + 1 : b + 2],
                                        st[:, b + 8 : b + 9], ALU.add, ALU.mult)

        esA = contextlib.ExitStack()
        xt = [esA.enter_context(nc.sbuf_tensor(f"xt{i}", [128, D], F32)) for i in range(2)]
        xn_bf = esA.enter_context(nc.sbuf_tensor("xn_bf", [128, D], BF16))
        xnT_sb = esA.enter_context(nc.sbuf_tensor("xnT_sb", [128, D], BF16))
        K_sb = esA.enter_context(nc.sbuf_tensor("K_sb", [128, D], BF16))
        V_sb = esA.enter_context(nc.sbuf_tensor("V_sb", [128, D], BF16))
        wk_bf = esA.enter_context(nc.sbuf_tensor("wk_bf", [128, DC, D], BF16))
        wv_bf = esA.enter_context(nc.sbuf_tensor("wv_bf", [128, DC, D], BF16))
        ps_kv = [esA.enter_context(nc.psum_tensor(f"ps_kv{b}", [128, 512], F32)) for b in range(4)]
        ps_ks = esA.enter_context(nc.psum_tensor("ps_ks", [128, H], F32))
        ps_rot = [esA.enter_context(nc.psum_tensor(f"ps_rot{b}", [128, 512], F32)) for b in range(2)]
        ps_tr = esA.enter_context(nc.psum_tensor("ps_tr", [128, 1024], BF16))
        bk_bf = esA.enter_context(nc.sbuf_tensor("bk_bf", [1, D], BF16))
        bv_bf = esA.enter_context(nc.sbuf_tensor("bv_bf", [1, D], BF16))

        with nc.Block() as blk:
            @blk.sync
            def _(sync):
                sync.dma_start(out=lut_s[:], in_=lut16).then_inc(s_misc, 16)
                sync.dma_start(out=dk_sb[:], in_=dk_cols).then_inc(s_misc, 16)
                sync.dma_start(out=dq_sb[:], in_=dq_cols).then_inc(s_misc, 16)
                sync.dma_start(out=eye_sb[:], in_=eye_in).then_inc(s_misc, 16)
                sync.dma_start(out=xt[0][0:1, 0:D], in_=bk).then_inc(s_misc, 16)
                sync.dma_start(out=xt[1][0:1, 0:D], in_=bv).then_inc(s_misc, 16)
                C["s_misc"] += 6 * 16
                for j in range(32):
                    src = wkT if j < 16 else wvT
                    dc = j % 16
                    if j < 2:
                        sync.wait_ge(s_sc, 2)
                    else:
                        sync.wait_ge(s_sc, 3 + (j - 2) + 1)
                    d = sync.dma_start(out=xt[j % 2][:], in_=src[dc * 128 : (dc + 1) * 128, :])
                    inc(d, f"s_x{j % 2}", s_xp[j % 2], 16)
                bdone(sync)

            @blk.gpsimd
            def _(gpsimd):
                gpsimd.memset(ones_bf[:], 1.0)
                gpsimd.memset(onesP_bf[:], 1.0)
                gpsimd.memset(onesC_bf[:], 1.0)
                gpsimd.memset(epsln[:], 1e-5)
                gpsimd.memset(epsat[:], 1e-6)
                i = gpsimd.drain()
                inc(i, "s_gp", s_gp, 1)
                g_dma_idx(gpsimd, 0)
                bdone(gpsimd)

            @blk.scalar
            def _(scalar):
                scalar.wait_ge(s_misc, 6 * 16)
                scalar.copy(bk_bf[:], xt[0][0:1, 0:D]).then_inc(s_sc, 1)
                scalar.copy(bv_bf[:], xt[1][0:1, 0:D]).then_inc(s_sc, 1)
                scalar.copy(lut_b[:], lut_s[:]).then_inc(s_sc, 1)
                C["s_sc"] += 3
                for j in range(32):
                    src = wk_bf if j < 16 else wv_bf
                    dc = j % 16
                    scalar.wait_ge(s_xp[j % 2], 16 * (j // 2 + 1))
                    i = scalar.copy(src[:, dc, :], xt[j % 2][:])
                    inc(i, "s_sc", s_sc, 1)
                bdone(scalar)

            @blk.tensor
            def _(tensor):
                tensor.wait_ge(s_sc, 5)
                tensor.wait_ge(s_gp, 1)
                i = tensor.matmul(ps_ks[:], onesP_bf[:], lut_b[:], start=True, stop=True)
                inc(i, "s_mm", s_mm, 1)
                bdone(tensor)

            @blk.vector
            def _(vector):
                vector.wait_ge(s_mm, 1)
                i = vector.tensor_copy(lutb[:], ps_ks[:])
                inc(i, "s_ve", s_ve, 1)
                bdone(vector)

        BB[0] += 1

        TPT = 11
        VA0, SA0, MA0 = C["s_ve"], C["s_sc"], C["s_mm"]
        XA = [C["s_x0"], C["s_x1"]]

        EVv, EVs = {}, {}
        _cv = VA0
        for _t in range(NT + 1):
            if _t < NT:
                _cv += 1; EVv[("sum", _t)] = _cv
            _u = _t - 1
            if 0 <= _u < NT:
                _cv += 1; EVv[("st", _u)] = _cv
                _cv += 1; EVv[("re", _u)] = _cv
                _cv += 1; EVv[("xn", _u)] = _cv
        _cs = SA0
        for _t in range(NT + 2):
            if _t < NT:
                _cs += 1; EVs[("pre", _t)] = _cs
            _u = _t - 1
            if 0 <= _u < NT:
                _cs += 1; EVs[("mid", _u)] = _cs
                _cs += 1; EVs[("post", _u)] = _cs
            _w = _t - 2
            if 0 <= _w < NT:
                for _g in range(2):
                    _cs += 1; EVs[("tr", _w, _g)] = _cs
                for _j in range(4):
                    _cs += 1; EVs[("k", _w, _j)] = _cs
                for _j in range(4):
                    _cs += 1; EVs[("v", _w, _j)] = _cs

        def vsumA(t):
            return EVv[("sum", t)]

        def vstA(u):
            return EVv[("st", u)]

        def vreA(u):
            return EVv[("re", u)]

        def vxnA(u):
            return EVv[("xn", u)]

        def spreA(t):
            return EVs[("pre", t)]

        def smidA(u):
            return EVs[("mid", u)]

        def spostA(u):
            return EVs[("post", u)]

        def strA(w, g):
            return EVs[("tr", w, g)]

        def skA(w, j):
            return EVs[("k", w, j)]

        def svA(w, j):
            return EVs[("v", w, j)]

        def mA(t, k):
            return MA0 + t * TPT + k

        with nc.Block() as blk:
            @blk.sync
            def _(sync):
                bwait(sync)
                for t in range(NT):
                    if t >= 2:
                        sync.wait_ge(s_ve, vxnA(t - 2))
                    d = sync.dma_start(out=xt[t % 2][:], in_=xb[t * 128 : (t + 1) * 128, :])
                    inc(d, f"s_x{t % 2}", s_xp[t % 2], 16)
                bdone(sync)

            @blk.vector
            def _(vector):
                bwait(vector)
                for t in range(NT + 1):
                    if t < NT:
                        vector.wait_ge(s_xp[t % 2], XA[t % 2] + 16 * (t // 2 + 1))
                        i = ln_vec_sum(vector, xt[t % 2], t % 2)
                        inc(i, "s_ve", s_ve, 1)
                        assert C["s_ve"] == vsumA(t)
                    u = t - 1
                    if 0 <= u < NT:
                        vector.wait_ge(s_sc, spreA(u))
                        i = ln_vec_stats(vector, u % 2)
                        inc(i, "s_ve", s_ve, 1)
                        vector.wait_ge(s_sc, smidA(u))
                        i = ln_vec_recip(vector, u % 2)
                        inc(i, "s_ve", s_ve, 1)
                        vector.wait_ge(s_sc, spostA(u))
                        if u > 0:
                            vector.wait_ge(s_mm, mA(u - 1, 2))
                        i = ln_vec_xn(vector, xt[u % 2], xn_bf, u % 2)
                        inc(i, "s_ve", s_ve, 1)
                        assert C["s_ve"] == vxnA(u)
                        for k in g_slice(GA, u, NT):
                            g_vec(vector, k)
                vector.wait_ge(s_mm, mA(NT - 1, 11))
                for b in range(4):
                    i = vector.tensor_copy(KV_sb[:, b * 512 : (b + 1) * 512], ps_kv[b][:])
                    inc(i, "s_ve", s_ve, 1)
                i = vector.tensor_copy(Ksum_sb[:], ps_ks[:])
                inc(i, "s_ve", s_ve, 1)
                bdone(vector)

            @blk.scalar
            def _(scalar):
                bwait(scalar)
                for t in range(NT + 2):
                    if t < NT:
                        scalar.wait_ge(s_ve, vsumA(t))
                        i = ln_scalar_pre(scalar, xt[t % 2], t % 2)
                        inc(i, "s_sc", s_sc, 1)
                        assert C["s_sc"] == spreA(t)
                    u = t - 1
                    if 0 <= u < NT:
                        scalar.wait_ge(s_ve, vstA(u))
                        i = ln_scalar_mid(scalar, u % 2)
                        inc(i, "s_sc", s_sc, 1)
                        scalar.wait_ge(s_ve, vreA(u))
                        i = ln_scalar_post(scalar, u % 2)
                        inc(i, "s_sc", s_sc, 1)
                    w = t - 2
                    if 0 <= w < NT:
                        for g in range(2):
                            scalar.wait_ge(s_mm, mA(w, 1 + g))
                            i = scalar.copy(xnT_sb[:, g * 1024 : (g + 1) * 1024], ps_tr[:])
                            inc(i, "s_sc", s_sc, 1)
                        for j in range(4):
                            scalar.wait_ge(s_mm, mA(w, 3 + j))
                            i = scalar.activation(K_sb[:, j * 512 : (j + 1) * 512], ps_rot[j % 2][:],
                                                  AF.Relu, scale=dk_sb[:, w : w + 1])
                            inc(i, "s_sc", s_sc, 1)
                        for j in range(4):
                            scalar.wait_ge(s_mm, mA(w, 7 + j))
                            i = scalar.copy(V_sb[:, j * 512 : (j + 1) * 512], ps_rot[j % 2][:])
                            inc(i, "s_sc", s_sc, 1)
                        assert C["s_sc"] == svA(w, 3)
                bdone(scalar)

            @blk.tensor
            def _(tensor):
                bwait(tensor)
                for t in range(NT):
                    tensor.wait_ge(s_ve, vxnA(t))
                    for g in range(2):
                        if g == 1:
                            tensor.wait_ge(s_sc, strA(t, 0))
                        elif t > 0:
                            tensor.wait_ge(s_sc, strA(t - 1, 1))
                        for kk in range(8):
                            dc = g * 8 + kk
                            i = tensor.transpose(ps_tr[:, kk * 128 : (kk + 1) * 128],
                                                 xn_bf[:, dc * 128 : (dc + 1) * 128], eye_sb[:])
                        inc(i, "s_mm", s_mm, 1)
                    tensor.wait_ge(s_sc, strA(t, 1))
                    for j in range(4):
                        if j >= 2:
                            tensor.wait_ge(s_sc, skA(t, j - 2))
                        elif t > 0:
                            tensor.wait_ge(s_sc, svA(t - 1, 2 + (j % 2)))
                        for dc in range(DC):
                            tensor.matmul(ps_rot[j % 2][:], xnT_sb[:, dc * 128 : (dc + 1) * 128],
                                          wk_bf[:, dc, j * 512 : (j + 1) * 512],
                                          start=(dc == 0), stop=False)
                        i = tensor.matmul(ps_rot[j % 2][:], onesP_bf[:],
                                          bk_bf[:, j * 512 : (j + 1) * 512], start=False, stop=True)
                        inc(i, "s_mm", s_mm, 1)
                    for j in range(4):
                        if j >= 2:
                            tensor.wait_ge(s_sc, svA(t, j - 2))
                        else:
                            tensor.wait_ge(s_sc, skA(t, 2 + (j % 2)))
                        for dc in range(DC):
                            tensor.matmul(ps_rot[j % 2][:], xnT_sb[:, dc * 128 : (dc + 1) * 128],
                                          wv_bf[:, dc, j * 512 : (j + 1) * 512],
                                          start=(dc == 0), stop=False)
                        i = tensor.matmul(ps_rot[j % 2][:], onesP_bf[:],
                                          bv_bf[:, j * 512 : (j + 1) * 512], start=False, stop=True)
                        inc(i, "s_mm", s_mm, 1)
                    tensor.wait_ge(s_sc, svA(t, 3))
                    for h in range(H):
                        b, k4 = h // 4, h % 4
                        tensor.matmul(ps_kv[b][:, k4 * 128 : (k4 + 1) * 128],
                                      K_sb[:, h * 128 : (h + 1) * 128],
                                      V_sb[:, h * 128 : (h + 1) * 128],
                                      start=(t == 0), stop=(t == NT - 1), skip_group_check=True)
                    for h in range(H):
                        i = tensor.matmul(ps_ks[:, h : h + 1], K_sb[:, h * 128 : (h + 1) * 128],
                                          onesC_bf[:], start=(t == 0), stop=(t == NT - 1),
                                          skip_group_check=True)
                    inc(i, "s_mm", s_mm, 1)
                bdone(tensor)

            @blk.gpsimd
            def _(gpsimd):
                bwait(gpsimd)
                for k in GA:
                    g_front(gpsimd, k)
                bdone(gpsimd)

        BB[0] += 1
        esA.close()

        arena = es0.enter_context(nc.sbuf_tensor("arena", [128, 76800], BF16))
        xnT_own = arena[:, 0:16384].rearrange("p (a b) -> p a b", a=DC)
        esB1 = contextlib.ExitStack()
        xtb1_0 = esB1.enter_context(nc.sbuf_tensor("xtb1_0", [128, D], F32))
        xtb1 = [xtb1_0, xtb1_0]
        xnb1 = arena[:, 16384:18432]
        ps_trb = esB1.enter_context(nc.psum_tensor("ps_trb", [128, 1024], BF16))

        VPT1, SPT1, TPT1 = 6, 3, 2
        VB0, SB0, MB0 = C["s_ve"], C["s_sc"], C["s_mm"]
        XB1 = [C["s_x0"], C["s_x1"]]

        def vB1(t, k):
            return VB0 + t * VPT1 + k

        def sB1(t, k):
            return SB0 + t * SPT1 + k

        def mB1(t, k):
            return MB0 + t * TPT1 + k

        with nc.Block() as blk:
            @blk.sync
            def _(sync):
                bwait(sync)
                for t in range(OT):
                    if t >= 1:
                        sync.wait_ge(s_ve, vB1(t - 1, 4))
                    d = sync.dma_start(out=xtb1[0][:], in_=xo[t * 128 : (t + 1) * 128, :])
                    inc(d, "s_x0", s_xp[0], 16)
                bdone(sync)

            @blk.scalar
            def _(scalar):
                bwait(scalar)
                for t in range(OT):
                    scalar.wait_ge(s_ve, vB1(t, 1))
                    i = ln_scalar_pre(scalar, xtb1[0], t % 2)
                    inc(i, "s_sc", s_sc, 1)
                    scalar.wait_ge(s_ve, vB1(t, 2))
                    i = ln_scalar_mid(scalar, t % 2)
                    inc(i, "s_sc", s_sc, 1)
                    scalar.wait_ge(s_ve, vB1(t, 3))
                    i = ln_scalar_post(scalar, t % 2)
                    inc(i, "s_sc", s_sc, 1)
                bdone(scalar)

            @blk.vector
            def _(vector):
                bwait(vector)
                for t in range(OT):
                    vector.wait_ge(s_xp[0], XB1[0] + 16 * (t + 1))
                    i = ln_vec_sum(vector, xtb1[0], t % 2)
                    inc(i, "s_ve", s_ve, 1)
                    vector.wait_ge(s_sc, sB1(t, 1))
                    i = ln_vec_stats(vector, t % 2)
                    inc(i, "s_ve", s_ve, 1)
                    vector.wait_ge(s_sc, sB1(t, 2))
                    i = ln_vec_recip(vector, t % 2)
                    inc(i, "s_ve", s_ve, 1)
                    vector.wait_ge(s_sc, sB1(t, 3))
                    i = ln_vec_xn(vector, xtb1[0], xnb1, t % 2)
                    inc(i, "s_ve", s_ve, 1)
                    for g in range(2):
                        vector.wait_ge(s_mm, mB1(t, 1 + g))
                        dst = xnT_own[:, g * 8 : (g + 1) * 8, t * 128 : (t + 1) * 128]
                        i = vector.tensor_copy(dst, ps_trb[:].rearrange("p (a b) -> p a b", a=8))
                        inc(i, "s_ve", s_ve, 1)
                    for k in g_slice(GB1, t, OT):
                        g_vec(vector, k)
                bdone(vector)

            @blk.tensor
            def _(tensor):
                bwait(tensor)
                for t in range(OT):
                    tensor.wait_ge(s_ve, vB1(t, 4))
                    for g in range(2):
                        if g == 1:
                            tensor.wait_ge(s_ve, vB1(t, 5))
                        elif t > 0:
                            tensor.wait_ge(s_ve, vB1(t - 1, 6))
                        for kk in range(8):
                            dc = g * 8 + kk
                            i = tensor.transpose(ps_trb[:, kk * 128 : (kk + 1) * 128],
                                                 xnb1[:, dc * 128 : (dc + 1) * 128], eye_sb[:])
                        inc(i, "s_mm", s_mm, 1)
                bdone(tensor)

            @blk.gpsimd
            def _(gpsimd):
                bwait(gpsimd)
                for k in GB1:
                    g_front(gpsimd, k)
                bdone(gpsimd)

        BB[0] += 1
        esB1.close()

        Q_sb = arena[:, 16384:32768].rearrange("p (a b) -> p a b", a=H)
        esB2 = contextlib.ExitStack()
        xtb2_0 = esB2.enter_context(nc.sbuf_tensor("xtb2_0", [128, D], F32))
        xtb2 = [xtb2_0, xtb2_0]
        bq_bf = esB2.enter_context(nc.sbuf_tensor("bq_bf", [1, D], BF16))
        wbig = arena[:, 32768:65536].rearrange("p (a b) -> p a b", a=DC)
        ps_q = [esB2.enter_context(nc.psum_tensor(f"ps_q{i}", [128, 512], F32)) for i in range(4)]

        VB2, SB2, MB2 = C["s_ve"], C["s_sc"], C["s_mm"]
        XB2 = [C["s_x0"], C["s_x1"]]
        with nc.Block() as blk:
            @blk.sync
            def _(sync):
                bwait(sync)
                d = sync.dma_start(out=xtb2[0][0:1, :], in_=bq)
                inc(d, "s_x0", s_xp[0], 16)
                for dc in range(DC):
                    sync.wait_ge(s_sc, SB2 + dc + 1)
                    d = sync.dma_start(out=xtb2[0][:], in_=wqT[dc * 128 : (dc + 1) * 128, :])
                    inc(d, "s_x0", s_xp[0], 16)
                bdone(sync)

            @blk.scalar
            def _(scalar):
                bwait(scalar)
                scalar.wait_ge(s_xp[0], XB2[0] + 16)
                i = scalar.copy(bq_bf[:], xtb2[0][0:1, :])
                inc(i, "s_sc", s_sc, 1)
                for dc in range(DC):
                    scalar.wait_ge(s_xp[0], XB2[0] + 16 * (dc + 2))
                    i = scalar.copy(wbig[:, dc, :], xtb2[0][:])
                    inc(i, "s_sc", s_sc, 1)
                bdone(scalar)

            @blk.tensor
            def _(tensor):
                bwait(tensor)
                tensor.wait_ge(s_sc, SB2 + DC + 1)
                u = 0
                for qc in range(H):
                    for h2 in range(2):
                        if u >= 4:
                            tensor.wait_ge(s_ve, VB2 + (u - 4) + 1)
                        pq = ps_q[u % 4]
                        for dc in range(DC):
                            tensor.matmul(pq[:], wbig[:, dc, qc * 128 : (qc + 1) * 128],
                                          xnT_own[:, dc, h2 * 512 : (h2 + 1) * 512],
                                          start=(dc == 0), stop=False)
                        i = tensor.matmul(pq[:], bq_bf[:, qc * 128 : (qc + 1) * 128],
                                          ones_bf[:, 0:512], start=False, stop=True)
                        inc(i, "s_mm", s_mm, 1)
                        u += 1
                bdone(tensor)

            @blk.vector
            def _(vector):
                bwait(vector)
                u = 0
                for qc in range(H):
                    for h2 in range(2):
                        vector.wait_ge(s_mm, MB2 + u + 1)
                        i = vector.tensor_scalar(Q_sb[:, qc, h2 * 512 : (h2 + 1) * 512],
                                                 ps_q[u % 4][:], 0.0, None, ALU.max)
                        inc(i, "s_ve", s_ve, 1)
                        u += 1
                        for k in g_slice(GB2, u - 1, 32):
                            g_vec(vector, k)
                bdone(vector)

            @blk.gpsimd
            def _(gpsimd):
                bwait(gpsimd)
                for k in GB2:
                    g_front(gpsimd, k)
                bdone(gpsimd)

        BB[0] += 1
        esB2.close()

        attnOT = arena[:, 32768:49152].rearrange("p (a b) -> p a b", a=DC)
        esB3 = contextlib.ExitStack()
        aO_sb = esB3.enter_context(nc.sbuf_tensor("aO_sb", [128, D], BF16))
        ps_z = esB3.enter_context(nc.psum_tensor("ps_z", [128, H], F32))
        ps_o = [esB3.enter_context(nc.psum_tensor(f"ps_o{i}", [128, 512], F32)) for i in range(2)]
        ps_tr3 = esB3.enter_context(nc.psum_tensor("ps_tr3", [128, 1024], BF16))

        VB3, SB3, MB3 = C["s_ve"], C["s_sc"], C["s_mm"]
        TPT3, VPT3, SPT3 = 7, 7, 2

        def mB3(t, k):
            return MB3 + t * TPT3 + k

        def vB3(t, k):
            return VB3 + t * VPT3 + k

        def sB3(t, k):
            return SB3 + t * SPT3 + k

        with nc.Block() as blk:
            @blk.tensor
            def _(tensor):
                bwait(tensor)
                for t in range(OT):
                    if t > 0:
                        tensor.wait_ge(s_sc, sB3(t - 1, 1))
                    for h in range(H):
                        i = tensor.matmul(ps_z[:, h : h + 1], Q_sb[:, h, t * 128 : (t + 1) * 128],
                                          Ksum_sb[:, h : h + 1], start=True, stop=True,
                                          skip_group_check=True)
                    inc(i, "s_mm", s_mm, 1)
                    for bb in range(4):
                        if bb >= 2:
                            tensor.wait_ge(s_ve, vB3(t, 2 + (bb - 2)))
                        elif t > 0:
                            tensor.wait_ge(s_ve, vB3(t - 1, 4 + bb))
                        for k4 in range(4):
                            h = bb * 4 + k4
                            i = tensor.matmul(ps_o[bb % 2][:, k4 * 128 : (k4 + 1) * 128],
                                              Q_sb[:, h, t * 128 : (t + 1) * 128],
                                              KV_sb[:, h * 128 : (h + 1) * 128],
                                              start=True, stop=True, skip_group_check=True)
                        inc(i, "s_mm", s_mm, 1)
                    tensor.wait_ge(s_ve, vB3(t, 5))
                    for g in range(2):
                        if g == 1:
                            tensor.wait_ge(s_ve, vB3(t, 6))
                        elif t > 0:
                            tensor.wait_ge(s_ve, vB3(t - 1, 7))
                        for kk in range(8):
                            mc = g * 8 + kk
                            i = tensor.transpose(ps_tr3[:, kk * 128 : (kk + 1) * 128],
                                                 aO_sb[:, mc * 128 : (mc + 1) * 128], eye_sb[:])
                        inc(i, "s_mm", s_mm, 1)
                bdone(tensor)

            @blk.scalar
            def _(scalar):
                bwait(scalar)
                for t in range(OT):
                    scalar.wait_ge(s_mm, mB3(t, 1))
                    i = scalar.activation(zt_sb[:], ps_z[:], AF.Relu, bias=epsat[:, 0:1],
                                          scale=dq_sb[:, t : t + 1])
                    inc(i, "s_sc", s_sc, 1)
                    scalar.wait_ge(s_ve, vB3(t, 1))
                    i = scalar.mul(zde_sb[:], zr_sb[:], dq_sb[:, t : t + 1])
                    inc(i, "s_sc", s_sc, 1)
                bdone(scalar)

            @blk.vector
            def _(vector):
                bwait(vector)
                for t in range(OT):
                    vector.wait_ge(s_sc, sB3(t, 1))
                    i = vector.reciprocal(zr_sb[:], zt_sb[:])
                    inc(i, "s_ve", s_ve, 1)
                    vector.wait_ge(s_sc, sB3(t, 2))
                    for bb in range(4):
                        vector.wait_ge(s_mm, mB3(t, 2 + bb))
                        for k4 in range(4):
                            h = bb * 4 + k4
                            vector.tensor_scalar(aO_sb[:, h * 128 : (h + 1) * 128],
                                                 ps_o[bb % 2][:, k4 * 128 : (k4 + 1) * 128],
                                                 zde_sb[:, h : h + 1], None, ALU.mult)
                        i = vector.drain()
                        inc(i, "s_ve", s_ve, 1)
                    for g in range(2):
                        vector.wait_ge(s_mm, mB3(t, 6 + g))
                        dst = attnOT[:, g * 8 : (g + 1) * 8, t * 128 : (t + 1) * 128]
                        i = vector.tensor_copy(dst, ps_tr3[:].rearrange("p (a b) -> p a b", a=8))
                        inc(i, "s_ve", s_ve, 1)
                    for k in g_slice(GB3, t, OT):
                        g_vec(vector, k)
                bdone(vector)

            @blk.gpsimd
            def _(gpsimd):
                bwait(gpsimd)
                for k in GB3:
                    g_front(gpsimd, k)
                bdone(gpsimd)

            @blk.sync
            def _(sync):
                bwait(sync)
                bdone(sync)

        BB[0] += 1
        esB3.close()

        woacc = arena[:, 49152:65536].rearrange("p (a b) -> p a b", a=OT)
        esB5a = contextlib.ExitStack()
        xtb5_0 = esB5a.enter_context(nc.sbuf_tensor("xtb5_0", [128, 1024], F32))
        xtb5 = [xtb5_0, xtb5_0]
        bo_bf = esB5a.enter_context(nc.sbuf_tensor("bo_bf", [1, D], BF16))
        wohalf = arena[:, 16384:32768].rearrange("p (a b) -> p a b", a=DC)
        ps_w = [esB5a.enter_context(nc.psum_tensor(f"ps_w{i}", [128, 1024], F32)) for i in range(2)]

        VB5a, SB5a, MB5a = C["s_ve"], C["s_sc"], C["s_mm"]
        XB5a = [C["s_x0"], C["s_x1"]]
        with nc.Block() as blk:
            @blk.sync
            def _(sync):
                bwait(sync)
                d = sync.dma_start(out=xtb5[0][0:1, :], in_=bo[0:1, 0:1024])
                inc(d, "s_x0", s_xp[0], 16)
                sync.wait_ge(s_sc, SB5a + 1)
                d = sync.dma_start(out=xtb5[0][0:1, :], in_=bo[0:1, 1024:2048])
                inc(d, "s_x0", s_xp[0], 16)
                for hf in range(2):
                    for dc in range(DC):
                        j = hf * DC + dc
                        sync.wait_ge(s_sc, SB5a + j + 2)
                        d = sync.dma_start(out=xtb5[0][:],
                                           in_=woT[dc * 128 : (dc + 1) * 128,
                                                   hf * 1024 : (hf + 1) * 1024])
                        inc(d, "s_x0", s_xp[0], 16)
                bdone(sync)

            @blk.scalar
            def _(scalar):
                bwait(scalar)
                scalar.wait_ge(s_xp[0], XB5a[0] + 16)
                i = scalar.copy(bo_bf[0:1, 0:1024], xtb5[0][0:1, :])
                inc(i, "s_sc", s_sc, 1)
                scalar.wait_ge(s_xp[0], XB5a[0] + 32)
                i = scalar.copy(bo_bf[0:1, 1024:2048], xtb5[0][0:1, :])
                inc(i, "s_sc", s_sc, 1)
                for hf in range(2):
                    for dc in range(DC):
                        j = hf * DC + dc
                        scalar.wait_ge(s_xp[0], XB5a[0] + 16 * (j + 3))
                        if hf == 1:
                            scalar.wait_ge(s_mm, MB5a + 8)
                        i = scalar.copy(wohalf[:, dc, :], xtb5[0][:])
                        inc(i, "s_sc", s_sc, 1)
                bdone(scalar)

            @blk.tensor
            def _(tensor):
                bwait(tensor)
                u = 0
                for hf in range(2):
                    tensor.wait_ge(s_sc, SB5a + DC * (hf + 1) + 2)
                    for t in range(OT):
                        if u >= 2:
                            tensor.wait_ge(s_ve, VB5a + (u - 2) + 1)
                        pw = ps_w[u % 2]
                        for mc in range(DC):
                            for j2 in range(2):
                                tensor.matmul(pw[:, j2 * 512 : (j2 + 1) * 512],
                                              attnOT[:, mc, t * 128 : (t + 1) * 128],
                                              wohalf[:, mc, j2 * 512 : (j2 + 1) * 512],
                                              start=(mc == 0), stop=False)
                        for j2 in range(2):
                            i = tensor.matmul(pw[:, j2 * 512 : (j2 + 1) * 512], onesP_bf[:],
                                              bo_bf[:, hf * 1024 + j2 * 512 : hf * 1024 + (j2 + 1) * 512],
                                              start=False, stop=True)
                        inc(i, "s_mm", s_mm, 1)
                        u += 1
                bdone(tensor)

            @blk.vector
            def _(vector):
                bwait(vector)
                u = 0
                for hf in range(2):
                    for t in range(OT):
                        for k in g_slice(GB5a, u, 16):
                            g_vec(vector, k)
                        vector.wait_ge(s_mm, MB5a + u + 1)
                        i = vector.tensor_copy(woacc[:, t, hf * 1024 : (hf + 1) * 1024], ps_w[u % 2][:])
                        inc(i, "s_ve", s_ve, 1)
                        u += 1
                bdone(vector)

            @blk.gpsimd
            def _(gpsimd):
                bwait(gpsimd)
                for k in GB5a:
                    g_front(gpsimd, k)
                bdone(gpsimd)

        BB[0] += 1
        esB5a.close()

        hT_a = arena[:, 16384:49152].rearrange("p (a b) -> p a b", a=32)
        hT_b = arena[:, 65536:76800].rearrange("p (a b) -> p a b", a=IC - 32)

        def hT_sl(ic, fsl):
            return hT_a[:, ic, fsl] if ic < 32 else hT_b[:, ic - 32, fsl]
        esB4 = contextlib.ExitStack()
        w1sb_ = esB4.enter_context(nc.sbuf_tensor("w1sb", [128, D], BF16))
        w2sb_ = esB4.enter_context(nc.sbuf_tensor("w2sb", [128, D], BF16))
        w1b = [w1sb_, KV_sb]
        w2b = [w2sb_, wt2]
        silu_sb = [esB4.enter_context(nc.sbuf_tensor(f"silu{i}", [128, 512], F32)) for i in range(2)]
        ps_g = [esB4.enter_context(nc.psum_tensor(f"ps_g{i}", [128, 1024], F32)) for i in range(2)]
        ps_u = [esB4.enter_context(nc.psum_tensor(f"ps_u{i}", [128, 1024], F32)) for i in range(2)]

        VB4, SB4, MB4 = C["s_ve"], C["s_sc"], C["s_mm"]
        XB4 = [C["s_x0"], C["s_x1"]]
        with nc.Block() as blk:
            @blk.sync
            def _(sync):
                bwait(sync)
                for ic in range(IC):
                    sync.wait_ge(s_go, 16 * (2 * ic + 2))
                    if ic >= 2:
                        sync.wait_ge(s_mm, MB4 + 4 * (ic - 1))
                    if ic >= 1:
                        sync.wait_ge(s_xp[0], XB4[0] + 16 * ic)
                        sync.wait_ge(s_xp[1], XB4[1] + 16 * ic)
                    d = sync.dma_start(out=w1b[ic % 2][:], in_=wh[0][ic, :, :])
                    inc(d, "s_x0", s_xp[0], 16)
                    d = sync.dma_start(out=w2b[ic % 2][:], in_=wh[1][ic, :, :])
                    inc(d, "s_x1", s_xp[1], 16)
                bdone(sync)

            @blk.tensor
            def _(tensor):
                bwait(tensor)
                for ic in range(IC):
                    tensor.wait_ge(s_xp[0], XB4[0] + 16 * (ic + 1))
                    tensor.wait_ge(s_xp[1], XB4[1] + 16 * (ic + 1))
                    for h2 in range(2):
                        if ic >= 2:
                            tensor.wait_ge(s_ve, VB4 + 2 * (ic - 2) + h2 + 1)
                        pg, pu = ps_g[ic % 2], ps_u[ic % 2]
                        for dc in range(DC):
                            i = tensor.matmul(pg[:, h2 * 512 : (h2 + 1) * 512],
                                              w1b[ic % 2][:, dc * 128 : (dc + 1) * 128],
                                              xnT_own[:, dc, h2 * 512 : (h2 + 1) * 512],
                                              start=(dc == 0), stop=(dc == DC - 1))
                        inc(i, "s_mm", s_mm, 1)
                        for dc in range(DC):
                            i = tensor.matmul(pu[:, h2 * 512 : (h2 + 1) * 512],
                                              w2b[ic % 2][:, dc * 128 : (dc + 1) * 128],
                                              xnT_own[:, dc, h2 * 512 : (h2 + 1) * 512],
                                              start=(dc == 0), stop=(dc == DC - 1))
                        inc(i, "s_mm", s_mm, 1)
                bdone(tensor)

            @blk.scalar
            def _(scalar):
                bwait(scalar)
                for ic in range(IC):
                    for h2 in range(2):
                        scalar.wait_ge(s_mm, MB4 + 4 * ic + 2 * h2 + 1)
                        if ic > 0:
                            scalar.wait_ge(s_ve, VB4 + 2 * (ic - 1) + h2 + 1)
                        i = scalar.activation(silu_sb[h2][:], ps_g[ic % 2][:, h2 * 512 : (h2 + 1) * 512],
                                              AF.Silu)
                        inc(i, "s_sc", s_sc, 1)
                bdone(scalar)

            @blk.vector
            def _(vector):
                bwait(vector)
                for ic in range(IC):
                    for h2 in range(2):
                        vector.wait_ge(s_sc, SB4 + 2 * ic + h2 + 1)
                        vector.wait_ge(s_mm, MB4 + 4 * ic + 2 * h2 + 2)
                        i = vector.tensor_tensor(hT_sl(ic, slice(h2 * 512, (h2 + 1) * 512)),
                                                 silu_sb[h2][:],
                                                 ps_u[ic % 2][:, h2 * 512 : (h2 + 1) * 512],
                                                 ALU.mult)
                        inc(i, "s_ve", s_ve, 1)
                    for k in g_slice(GB4, ic, IC):
                        g_vec(vector, k)
                bdone(vector)

            @blk.gpsimd
            def _(gpsimd):
                bwait(gpsimd)
                for k in GB4:
                    g_front(gpsimd, k)
                bdone(gpsimd)

        BB[0] += 1
        esB4.close()

        esB5b = contextlib.ExitStack()
        w3qb = [arena[:, i * 512 : (i + 1) * 512] for i in range(2)]
        xo8 = [esB5b.enter_context(nc.sbuf_tensor(f"xo8_{i}", [128, 512], F32)) for i in range(8)]
        ps8 = [esB5b.enter_context(nc.psum_tensor(f"ps8_{i}", [128, 512], F32)) for i in range(8)]

        VB5, SB5, MB5 = C["s_ve"], C["s_sc"], C["s_mm"]
        XB5, XO0 = C["s_x0"], C["s_xo"]

        def fB5(j):
            return MB5 + (j // IC) * (IC + 1) + 2 + (j % IC)

        with nc.Block() as blk:
            @blk.scalar
            def _(scalar):
                bwait(scalar)
                bdone(scalar)

            @blk.sync
            def _(sync):
                bwait(sync)
                for q in range(4):
                    for ic in range(IC):
                        j = IC * q + ic
                        sync.wait_ge(s_go, 16 * (2 * IC + ic + 1))
                        if j >= 2:
                            sync.wait_ge(s_mm, fB5(j - 2))
                        if j >= 1:
                            sync.wait_ge(s_xp[0], XB5 + 16 * j)
                        d = sync.dma_start(out=w3qb[j % 2],
                                           in_=wh[2][ic, :, q * 512 : (q + 1) * 512])
                        inc(d, "s_x0", s_xp[0], 16)
                    for t in range(8):
                        m = 8 * q + t
                        if q > 0:
                            n = (8 * (q - 1) + t - t % 2) // 2 + 1
                            sync.wait_ge(s_oa[t % 2], 16 * n)
                        if m >= 1:
                            sync.wait_ge(s_xo, XO0 + 16 * m)
                        d = sync.dma_start(out=xo8[t][:],
                                           in_=xo[t * 128 : (t + 1) * 128, q * 512 : (q + 1) * 512])
                        inc(d, "s_xo", s_xo, 16)
                bdone(sync)

            @blk.tensor
            def _(tensor):
                bwait(tensor)
                for q in range(4):
                    if q > 0:
                        tensor.wait_ge(s_ve, VB5 + 8 * q)
                    for t in range(8):
                        i = tensor.matmul(ps8[t][:], eye_sb[:],
                                          woacc[:, t, q * 512 : (q + 1) * 512],
                                          start=True, stop=False, skip_group_check=True)
                    inc(i, "s_mm", s_mm, 1)
                    for ic in range(IC):
                        j = IC * q + ic
                        tensor.wait_ge(s_xp[0], XB5 + 16 * (j + 1))
                        for t in range(8):
                            i = tensor.matmul(ps8[t][:],
                                              hT_sl(ic, slice(t * 128, (t + 1) * 128)),
                                              w3qb[j % 2], start=False, stop=(ic == IC - 1),
                                              skip_group_check=True)
                        inc(i, "s_mm", s_mm, 1)
                bdone(tensor)

            @blk.vector
            def _(vector):
                bwait(vector)
                for q in range(4):
                    vector.wait_ge(s_mm, MB5 + (IC + 1) * (q + 1))
                    for t in range(8):
                        vector.wait_ge(s_xo, XO0 + 16 * (8 * q + t + 1))
                        i = vector.tensor_tensor(xo8[t][:], ps8[t][:], xo8[t][:], ALU.add)
                        inc(i, "s_ve", s_ve, 1)
                bdone(vector)

            @blk.gpsimd
            def _(gpsimd):
                bwait(gpsimd)
                for q in range(4):
                    for t in range(8):
                        gpsimd.wait_ge(s_ve, VB5 + 8 * q + t + 1)
                        n = (8 * q + t - t % 2) // 2
                        if n > 0:
                            gpsimd.wait_ge(s_oa[t % 2], 16 * n)
                        d = gpsimd.dma_start(out=y[t * 128 : (t + 1) * 128, q * 512 : (q + 1) * 512],
                                             in_=xo8[t][:])
                        inc(d, f"s_out{t % 2}", s_oa[t % 2], 16)
                gpsimd.wait_ge(s_oa[0], 16 * 16)
                gpsimd.wait_ge(s_oa[1], 16 * 16)
                bdone(gpsimd)

        BB[0] += 1
        esB5b.close()

    return nc


_NC_CACHE = [None]


def kernel(L0_attn_x, L0_attn_Wq, L0_attn_bq, L0_attn_Wk, L0_attn_bk,
           L0_attn_Wv, L0_attn_bv, L0_attn_Wo, L0_attn_bo,
           L0_ffn_lut, L0_ffn_gate_idx, L0_ffn_up_idx, L0_ffn_down_idx):
    x = np.ascontiguousarray(np.asarray(L0_attn_x, np.float32))

    def f32(a):
        return np.ascontiguousarray(np.asarray(a, np.float32))

    wqT = f32(np.asarray(L0_attn_Wq).T)
    wkT = f32(np.asarray(L0_attn_Wk).T)
    wvT = f32(np.asarray(L0_attn_Wv).T)
    woT = f32(np.asarray(L0_attn_Wo).T)
    bq_ = f32(L0_attn_bq).reshape(1, D)
    bk_ = f32(L0_attn_bk).reshape(1, D)
    bv_ = f32(L0_attn_bv).reshape(1, D)
    bo_ = f32(L0_attn_bo).reshape(1, D)
    lut = f32(L0_ffn_lut).reshape(1, 16)

    gidx = np.asarray(L0_ffn_gate_idx, np.int32)
    uidx = np.asarray(L0_ffn_up_idx, np.int32)
    didx = np.asarray(L0_ffn_down_idx, np.int32)
    g1 = np.ascontiguousarray(
        gidx.astype(np.int8).reshape(IC, 128, DC, 128).transpose(0, 3, 2, 1)).reshape(IC, 128, D)
    g2 = np.ascontiguousarray(
        uidx.astype(np.int8).reshape(IC, 128, DC, 128).transpose(0, 3, 2, 1)).reshape(IC, 128, D)
    d3 = np.ascontiguousarray(didx.astype(np.int8).T).reshape(IC, 128, D)

    pos = np.arange(S, dtype=np.float64)
    dec = np.exp(-pos / S).astype(np.float32)
    dk_cols = np.ascontiguousarray(dec.reshape(NT, 128).T)
    eye = np.eye(128).astype(ml_dtypes.bfloat16)

    if _NC_CACHE[0] is None:
        _NC_CACHE[0] = build()
    nc = _NC_CACHE[0]

    in_maps = []
    for c in range(8):
        b, q = c // 4, c % 4
        in_maps.append({
            "xb": np.ascontiguousarray(x[b]),
            "xo": np.ascontiguousarray(x[b, q * T : (q + 1) * T]),
            "wkT": wkT, "wvT": wvT, "wqT": wqT, "woT": woT,
            "bk": bk_, "bv": bv_, "bq": bq_, "bo": bo_,
            "g1": g1, "g2": g2, "d3": d3,
            "lut16": lut,
            "dk_cols": dk_cols,
            "dq_cols": np.ascontiguousarray(dk_cols[:, q * OT : (q + 1) * OT]),
            "eye_in": eye,
        })
    res = run_bass_kernel_spmd(nc, in_maps, core_ids=list(range(8)))
    out = np.empty((2, S, D), np.float32)
    for c in range(8):
        b, q = c // 4, c % 4
        out[b, q * T : (q + 1) * T] = np.asarray(res.results[c]["y"], np.float32)
    return out


# version 15
# speedup vs baseline: 1.0948x; 1.0948x over previous
import contextlib

import numpy as np
import ml_dtypes

import concourse.bass as bass
import concourse.mybir as mybir
from concourse.bass_utils import run_bass_kernel_spmd

F32 = mybir.dt.float32
BF16 = mybir.dt.bfloat16
I8 = mybir.dt.int8
AF = mybir.ActivationFunctionType
ALU = mybir.AluOpType
AX = mybir.AxisListType.X

D = 2048
H = 16
S = 4096
T = 1024
NT = 32
OT = 8
DC = 16
IC = 43
NITER = 3 * IC


def g_mat_ic(k):
    if k < 2 * IC:
        return (k % 2, k // 2)
    return (2, k - 2 * IC)


def build():
    nc = bass.Bass("TRN2", target_bir_lowering=False)

    xb = nc.dram_tensor("xb", [S, D], F32, kind="ExternalInput").ap()
    xo = nc.dram_tensor("xo", [T, D], F32, kind="ExternalInput").ap()
    wkT = nc.dram_tensor("wkT", [D, D], F32, kind="ExternalInput").ap()
    wvT = nc.dram_tensor("wvT", [D, D], F32, kind="ExternalInput").ap()
    wqT = nc.dram_tensor("wqT", [D, D], F32, kind="ExternalInput").ap()
    woT = nc.dram_tensor("woT", [D, D], F32, kind="ExternalInput").ap()
    bk = nc.dram_tensor("bk", [1, D], F32, kind="ExternalInput").ap()
    bv = nc.dram_tensor("bv", [1, D], F32, kind="ExternalInput").ap()
    bq = nc.dram_tensor("bq", [1, D], F32, kind="ExternalInput").ap()
    bo = nc.dram_tensor("bo", [1, D], F32, kind="ExternalInput").ap()
    g1 = nc.dram_tensor("g1", [IC, 128, D], I8, kind="ExternalInput").ap()
    g2 = nc.dram_tensor("g2", [IC, 128, D], I8, kind="ExternalInput").ap()
    d3 = nc.dram_tensor("d3", [IC, 128, D], I8, kind="ExternalInput").ap()
    lut16 = nc.dram_tensor("lut16", [1, 16], F32, kind="ExternalInput").ap()
    dk_cols = nc.dram_tensor("dk_cols", [128, NT], F32, kind="ExternalInput").ap()
    dq_cols = nc.dram_tensor("dq_cols", [128, OT], F32, kind="ExternalInput").ap()
    eye_in = nc.dram_tensor("eye_in", [128, 128], BF16, kind="ExternalInput").ap()
    y = nc.dram_tensor("y", [T, D], F32, kind="ExternalOutput").ap()
    wh = [nc.dram_tensor(f"w{m}h", [IC, 128, D], BF16, kind="Internal").ap() for m in range(3)]
    gsrc = [g1, g2, d3]

    C = {}

    def mk_sem(es, name):
        h = es.enter_context(nc.semaphore(name))
        C[name] = 0
        return h

    def inc(instr, name, h, n):
        C[name] += n
        instr.then_inc(h, n)
        return C[name]

    BB = [0]

    es0 = contextlib.ExitStack()
    with es0:
        ones_bf = es0.enter_context(nc.sbuf_tensor("ones_bf", [1, 1024], BF16))
        onesP_bf = es0.enter_context(nc.sbuf_tensor("onesP_bf", [1, 128], BF16))
        onesC_bf = es0.enter_context(nc.sbuf_tensor("onesC_bf", [128, 1], BF16))
        eye_sb = es0.enter_context(nc.sbuf_tensor("eye_sb", [128, 128], BF16))
        lut_s = es0.enter_context(nc.sbuf_tensor("lut_s", [1, 16], F32))
        lut_b = es0.enter_context(nc.sbuf_tensor("lut_b", [1, 16], BF16))
        lutb = es0.enter_context(nc.sbuf_tensor("lutb", [128, 16], F32))
        dk_sb = es0.enter_context(nc.sbuf_tensor("dk_sb", [128, NT], F32))
        dq_sb = es0.enter_context(nc.sbuf_tensor("dq_sb", [128, OT], F32))
        epsln = es0.enter_context(nc.sbuf_tensor("epsln", [128, 1], F32))
        epsat = es0.enter_context(nc.sbuf_tensor("epsat", [128, 1], F32))
        st = es0.enter_context(nc.sbuf_tensor("st", [128, 32], F32))
        gi = [es0.enter_context(nc.sbuf_tensor(f"gi{i}", [128, D], I8)) for i in range(2)]
        gbs = [es0.enter_context(nc.sbuf_tensor(f"gb{i}", [128, D], BF16)) for i in range(2)]
        wts = [es0.enter_context(nc.sbuf_tensor(f"wt{i}", [128, D], BF16)) for i in range(2)]
        planes = [es0.enter_context(nc.sbuf_tensor(f"plane{i}", [128, D], BF16)) for i in range(2)]
        wt2 = es0.enter_context(nc.sbuf_tensor("wt2", [128, D], BF16))

        KV_sb = es0.enter_context(nc.sbuf_tensor("KV_sb", [128, H * 128], BF16))
        Ksum_sb = es0.enter_context(nc.sbuf_tensor("Ksum_sb", [128, H], BF16))
        zt_sb = es0.enter_context(nc.sbuf_tensor("zt_sb", [128, H], F32))
        zr_sb = es0.enter_context(nc.sbuf_tensor("zr_sb", [128, H], F32))
        zde_sb = es0.enter_context(nc.sbuf_tensor("zde_sb", [128, H], F32))

        s_misc = mk_sem(es0, "s_misc")
        s_xp = [mk_sem(es0, "s_x0"), mk_sem(es0, "s_x1")]
        s_xo = mk_sem(es0, "s_xo")
        s_ip = [mk_sem(es0, "s_i0"), mk_sem(es0, "s_i1")]
        s_mm = mk_sem(es0, "s_mm")
        s_ve = mk_sem(es0, "s_ve")
        s_sc = mk_sem(es0, "s_sc")
        s_gp = mk_sem(es0, "s_gp")
        s_gc = mk_sem(es0, "s_gc")
        s_gv = mk_sem(es0, "s_gv")
        s_vv = mk_sem(es0, "s_vv")
        s_go = mk_sem(es0, "s_go")
        s_oa = [mk_sem(es0, "s_out0"), mk_sem(es0, "s_out1")]
        s_ph = mk_sem(es0, "s_ph")
        s_pl = mk_sem(es0, "s_pl")
        s_pm = mk_sem(es0, "s_pm")

        def bwait(e):
            if BB[0] > 0:
                e.wait_ge(s_ph, 5 * BB[0])

        def bdone(e):
            e.sem_inc(s_ph, 1)

        def g_dma_idx(eng, k):
            m, ic = g_mat_ic(k)
            if k >= 2:
                eng.wait_ge(s_gc, VEC_IDX[k - 2] + 1)
            d = eng.dma_start(out=gi[k % 2][:], in_=gsrc[m][ic, :, :])
            inc(d, f"s_i{k % 2}", s_ip[k % 2], 16)

        def g_cast(eng, k):
            vi = VEC_IDX[k]
            eng.wait_ge(s_ip[k % 2], 16 * (k // 2 + 1))
            if k >= 2:
                eng.wait_ge(s_gv, VEC_IDX[k - 2] + 1)
            i = eng.tensor_copy(gbs[k % 2][:], gi[k % 2][:])
            inc(i, "s_gc", s_gc, 1)

        def g_front(gpsimd, k):
            if k + 1 < NITER:
                g_dma_idx(gpsimd, k + 1)
            g_cast(gpsimd, k)
            if k > 0:
                g_out(gpsimd, k - 1)
            if k == NITER - 1:
                g_out(gpsimd, k)

        def g_vec(vector, k):
            if k in GP_OWN:
                return
            vi = VEC_IDX[k]
            gb = gbs[k % 2]
            wt = wts[k % 2]
            vector.wait_ge(s_gc, vi + 1)
            if ASSIST_LIST and k > ASSIST_LIST[-1]:
                vector.wait_ge(s_pm, 16 * len(ASSIST_LIST))
            if k > 1:
                vector.wait_ge(s_go, 16 * (k - 1))
            vector.tensor_scalar(wt[:], gb[:], 0.0, lutb[:, 0:1], ALU.is_equal, ALU.mult)
            for v in range(1, 16):
                vector.tensor_scalar(planes[v % 2][:], gb[:], float(v), lutb[:, v : v + 1],
                                     ALU.is_equal, ALU.mult)
                vector.drain()
                vector.tensor_tensor(wt[:], wt[:], planes[v % 2][:], ALU.add)
            i = vector.drain()
            inc(i, "s_gv", s_gv, 1)

        def g_out(gpsimd, k):
            m, ic = g_mat_ic(k)
            gpsimd.wait_ge(s_gv, VEC_IDX[k] + 1)
            if k > 0:
                gpsimd.wait_ge(s_go, 16 * k)
            d = gpsimd.dma_start(out=wh[m][ic, :, :], in_=wts[k % 2][:])
            inc(d, "s_go", s_go, 16)

        GP_OWN = set()
        VEC_IDX = {}
        _vi = 0
        for _k in range(NITER):
            if _k not in GP_OWN:
                VEC_IDX[_k] = _vi
                _vi += 1
        NVEC = _vi

        GA = list(range(0, 30))
        GB1 = list(range(30, 40))
        GB2 = list(range(40, 62))
        GB3 = list(range(62, 84))
        GB5a = list(range(84, 109))
        GB4 = list(range(109, NITER))

        def g_slice(lst, i, n):
            return lst[len(lst) * i // n : len(lst) * (i + 1) // n]

        ASSIST_LIST = GB1 + GB2 + GB3 + GB5a
        AORD = {k: a for a, k in enumerate(ASSIST_LIST)}
        PS_ACC = [None]

        def g_vec_a(vector, k):
            a = AORD[k]
            gb = gbs[k % 2]
            vector.wait_ge(s_gc, VEC_IDX[k] + 1)
            for v in range(16):
                p = a * 16 + v
                if p >= 2:
                    vector.wait_ge(s_pm, p - 1)
                i = vector.tensor_scalar(planes[v % 2][:], gb[:], float(v),
                                         lutb[:, v : v + 1], ALU.is_equal, ALU.mult)
                inc(i, "s_pl", s_pl, 1)
            vector.wait_ge(s_pm, (a + 1) * 16)
            if k > 1:
                vector.wait_ge(s_go, 16 * (k - 1))
            i = vector.tensor_copy(wts[k % 2][:], PS_ACC[0][:])
            inc(i, "s_gv", s_gv, 1)

        def g_pe(tensor, k):
            a = AORD[k]
            ps_acc = PS_ACC[0]
            for v in range(16):
                tensor.wait_ge(s_pl, a * 16 + v + 1)
                if v == 0:
                    tensor.wait_ge(s_gv, k)
                for j4 in range(4):
                    i = tensor.matmul(ps_acc[:, j4 * 512 : (j4 + 1) * 512], eye_sb[:],
                                      planes[v % 2][:, j4 * 512 : (j4 + 1) * 512],
                                      start=(v == 0), stop=(v == 15), skip_group_check=True)
                inc(i, "s_pm", s_pm, 1)

        def ln_vec_sum(vector, x_t, pb):
            b = 16 * pb
            return vector.reduce_sum(st[:, b : b + 1], x_t[:], axis=AX)

        def ln_scalar_pre(scalar, x_t, pb):
            b = 16 * pb
            scalar.activation(wt2[:], x_t[:], AF.Square, accum_out=st[:, b + 3 : b + 4])
            scalar.mul(st[:, b + 1 : b + 2], st[:, b : b + 1], -1.0 / D)
            return scalar.drain()

        def ln_vec_stats(vector, pb):
            b = 16 * pb
            vector.tensor_tensor(st[:, b + 4 : b + 5], st[:, b : b + 1], st[:, b : b + 1], ALU.mult)
            vector.drain()
            return vector.scalar_tensor_tensor(st[:, b + 5 : b + 6], st[:, b + 3 : b + 4], float(D),
                                               st[:, b + 4 : b + 5], ALU.mult, ALU.subtract)

        def ln_scalar_mid(scalar, pb):
            b = 16 * pb
            return scalar.activation(st[:, b + 6 : b + 7], st[:, b + 5 : b + 6], AF.Sqrt,
                                     bias=epsln[:, 0:1], scale=1.0 / (float(D) * float(D)))

        def ln_vec_recip(vector, pb):
            b = 16 * pb
            return vector.reciprocal(st[:, b + 7 : b + 8], st[:, b + 6 : b + 7])

        def ln_scalar_post(scalar, pb):
            b = 16 * pb
            return scalar.copy(st[:, b + 8 : b + 9], st[:, b + 7 : b + 8])

        def ln_vec_xn(vector, x_t, xn_out, pb):
            b = 16 * pb
            return vector.tensor_scalar(xn_out[:], x_t[:], st[:, b + 1 : b + 2],
                                        st[:, b + 8 : b + 9], ALU.add, ALU.mult)

        esA = contextlib.ExitStack()
        xt = [esA.enter_context(nc.sbuf_tensor(f"xt{i}", [128, D], F32)) for i in range(2)]
        xn_bf = esA.enter_context(nc.sbuf_tensor("xn_bf", [128, D], BF16))
        xnT_sb = esA.enter_context(nc.sbuf_tensor("xnT_sb", [128, D], BF16))
        K_sb = esA.enter_context(nc.sbuf_tensor("K_sb", [128, D], BF16))
        V_sb = esA.enter_context(nc.sbuf_tensor("V_sb", [128, D], BF16))
        wk_bf = esA.enter_context(nc.sbuf_tensor("wk_bf", [128, DC, D], BF16))
        wv_bf = esA.enter_context(nc.sbuf_tensor("wv_bf", [128, DC, D], BF16))
        ps_kv = [esA.enter_context(nc.psum_tensor(f"ps_kv{b}", [128, 512], F32)) for b in range(4)]
        ps_ks = esA.enter_context(nc.psum_tensor("ps_ks", [128, H], F32))
        ps_rot = [esA.enter_context(nc.psum_tensor(f"ps_rot{b}", [128, 512], F32)) for b in range(2)]
        ps_tr = esA.enter_context(nc.psum_tensor("ps_tr", [128, 1024], BF16))
        bk_bf = esA.enter_context(nc.sbuf_tensor("bk_bf", [1, D], BF16))
        bv_bf = esA.enter_context(nc.sbuf_tensor("bv_bf", [1, D], BF16))

        with nc.Block() as blk:
            @blk.sync
            def _(sync):
                sync.dma_start(out=lut_s[:], in_=lut16).then_inc(s_misc, 16)
                sync.dma_start(out=dk_sb[:], in_=dk_cols).then_inc(s_misc, 16)
                sync.dma_start(out=dq_sb[:], in_=dq_cols).then_inc(s_misc, 16)
                sync.dma_start(out=eye_sb[:], in_=eye_in).then_inc(s_misc, 16)
                sync.dma_start(out=xt[0][0:1, 0:D], in_=bk).then_inc(s_misc, 16)
                sync.dma_start(out=xt[1][0:1, 0:D], in_=bv).then_inc(s_misc, 16)
                C["s_misc"] += 6 * 16
                for j in range(32):
                    src = wkT if j < 16 else wvT
                    dc = j % 16
                    if j < 2:
                        sync.wait_ge(s_sc, 2)
                    else:
                        sync.wait_ge(s_sc, 3 + (j - 2) + 1)
                    d = sync.dma_start(out=xt[j % 2][:], in_=src[dc * 128 : (dc + 1) * 128, :])
                    inc(d, f"s_x{j % 2}", s_xp[j % 2], 16)
                bdone(sync)

            @blk.gpsimd
            def _(gpsimd):
                gpsimd.memset(ones_bf[:], 1.0)
                gpsimd.memset(onesP_bf[:], 1.0)
                gpsimd.memset(onesC_bf[:], 1.0)
                gpsimd.memset(epsln[:], 1e-5)
                gpsimd.memset(epsat[:], 1e-6)
                i = gpsimd.drain()
                inc(i, "s_gp", s_gp, 1)
                g_dma_idx(gpsimd, 0)
                bdone(gpsimd)

            @blk.scalar
            def _(scalar):
                scalar.wait_ge(s_misc, 6 * 16)
                scalar.copy(bk_bf[:], xt[0][0:1, 0:D]).then_inc(s_sc, 1)
                scalar.copy(bv_bf[:], xt[1][0:1, 0:D]).then_inc(s_sc, 1)
                scalar.copy(lut_b[:], lut_s[:]).then_inc(s_sc, 1)
                C["s_sc"] += 3
                for j in range(32):
                    src = wk_bf if j < 16 else wv_bf
                    dc = j % 16
                    scalar.wait_ge(s_xp[j % 2], 16 * (j // 2 + 1))
                    i = scalar.copy(src[:, dc, :], xt[j % 2][:])
                    inc(i, "s_sc", s_sc, 1)
                bdone(scalar)

            @blk.tensor
            def _(tensor):
                tensor.wait_ge(s_sc, 5)
                tensor.wait_ge(s_gp, 1)
                i = tensor.matmul(ps_ks[:], onesP_bf[:], lut_b[:], start=True, stop=True)
                inc(i, "s_mm", s_mm, 1)
                bdone(tensor)

            @blk.vector
            def _(vector):
                vector.wait_ge(s_mm, 1)
                i = vector.tensor_copy(lutb[:], ps_ks[:])
                inc(i, "s_ve", s_ve, 1)
                bdone(vector)

        BB[0] += 1

        TPT = 11
        VA0, SA0, MA0 = C["s_ve"], C["s_sc"], C["s_mm"]
        XA = [C["s_x0"], C["s_x1"]]

        EVv, EVs = {}, {}
        _cv = VA0
        for _t in range(NT + 1):
            if _t < NT:
                _cv += 1; EVv[("sum", _t)] = _cv
            _u = _t - 1
            if 0 <= _u < NT:
                _cv += 1; EVv[("st", _u)] = _cv
                _cv += 1; EVv[("re", _u)] = _cv
                _cv += 1; EVv[("xn", _u)] = _cv
        _cs = SA0
        for _t in range(NT + 2):
            if _t < NT:
                _cs += 1; EVs[("pre", _t)] = _cs
            _u = _t - 1
            if 0 <= _u < NT:
                _cs += 1; EVs[("mid", _u)] = _cs
                _cs += 1; EVs[("post", _u)] = _cs
            _w = _t - 2
            if 0 <= _w < NT:
                for _g in range(2):
                    _cs += 1; EVs[("tr", _w, _g)] = _cs
                for _j in range(4):
                    _cs += 1; EVs[("k", _w, _j)] = _cs
                for _j in range(4):
                    _cs += 1; EVs[("v", _w, _j)] = _cs

        def vsumA(t):
            return EVv[("sum", t)]

        def vstA(u):
            return EVv[("st", u)]

        def vreA(u):
            return EVv[("re", u)]

        def vxnA(u):
            return EVv[("xn", u)]

        def spreA(t):
            return EVs[("pre", t)]

        def smidA(u):
            return EVs[("mid", u)]

        def spostA(u):
            return EVs[("post", u)]

        def strA(w, g):
            return EVs[("tr", w, g)]

        def skA(w, j):
            return EVs[("k", w, j)]

        def svA(w, j):
            return EVs[("v", w, j)]

        def mA(t, k):
            return MA0 + t * TPT + k

        with nc.Block() as blk:
            @blk.sync
            def _(sync):
                bwait(sync)
                for t in range(NT):
                    if t >= 2:
                        sync.wait_ge(s_ve, vxnA(t - 2))
                    d = sync.dma_start(out=xt[t % 2][:], in_=xb[t * 128 : (t + 1) * 128, :])
                    inc(d, f"s_x{t % 2}", s_xp[t % 2], 16)
                bdone(sync)

            @blk.vector
            def _(vector):
                bwait(vector)
                for t in range(NT + 1):
                    if t < NT:
                        vector.wait_ge(s_xp[t % 2], XA[t % 2] + 16 * (t // 2 + 1))
                        i = ln_vec_sum(vector, xt[t % 2], t % 2)
                        inc(i, "s_ve", s_ve, 1)
                        assert C["s_ve"] == vsumA(t)
                    u = t - 1
                    if 0 <= u < NT:
                        vector.wait_ge(s_sc, spreA(u))
                        i = ln_vec_stats(vector, u % 2)
                        inc(i, "s_ve", s_ve, 1)
                        vector.wait_ge(s_sc, smidA(u))
                        i = ln_vec_recip(vector, u % 2)
                        inc(i, "s_ve", s_ve, 1)
                        vector.wait_ge(s_sc, spostA(u))
                        if u > 0:
                            vector.wait_ge(s_mm, mA(u - 1, 2))
                        i = ln_vec_xn(vector, xt[u % 2], xn_bf, u % 2)
                        inc(i, "s_ve", s_ve, 1)
                        assert C["s_ve"] == vxnA(u)
                        for k in g_slice(GA, u, NT):
                            g_vec(vector, k)
                vector.wait_ge(s_mm, mA(NT - 1, 11))
                for b in range(4):
                    i = vector.tensor_copy(KV_sb[:, b * 512 : (b + 1) * 512], ps_kv[b][:])
                    inc(i, "s_ve", s_ve, 1)
                i = vector.tensor_copy(Ksum_sb[:], ps_ks[:])
                inc(i, "s_ve", s_ve, 1)
                bdone(vector)

            @blk.scalar
            def _(scalar):
                bwait(scalar)
                for t in range(NT + 2):
                    if t < NT:
                        scalar.wait_ge(s_ve, vsumA(t))
                        i = ln_scalar_pre(scalar, xt[t % 2], t % 2)
                        inc(i, "s_sc", s_sc, 1)
                        assert C["s_sc"] == spreA(t)
                    u = t - 1
                    if 0 <= u < NT:
                        scalar.wait_ge(s_ve, vstA(u))
                        i = ln_scalar_mid(scalar, u % 2)
                        inc(i, "s_sc", s_sc, 1)
                        scalar.wait_ge(s_ve, vreA(u))
                        i = ln_scalar_post(scalar, u % 2)
                        inc(i, "s_sc", s_sc, 1)
                    w = t - 2
                    if 0 <= w < NT:
                        for g in range(2):
                            scalar.wait_ge(s_mm, mA(w, 1 + g))
                            i = scalar.copy(xnT_sb[:, g * 1024 : (g + 1) * 1024], ps_tr[:])
                            inc(i, "s_sc", s_sc, 1)
                        for j in range(4):
                            scalar.wait_ge(s_mm, mA(w, 3 + j))
                            i = scalar.activation(K_sb[:, j * 512 : (j + 1) * 512], ps_rot[j % 2][:],
                                                  AF.Relu, scale=dk_sb[:, w : w + 1])
                            inc(i, "s_sc", s_sc, 1)
                        for j in range(4):
                            scalar.wait_ge(s_mm, mA(w, 7 + j))
                            i = scalar.copy(V_sb[:, j * 512 : (j + 1) * 512], ps_rot[j % 2][:])
                            inc(i, "s_sc", s_sc, 1)
                        assert C["s_sc"] == svA(w, 3)
                bdone(scalar)

            @blk.tensor
            def _(tensor):
                bwait(tensor)
                for t in range(NT):
                    tensor.wait_ge(s_ve, vxnA(t))
                    for g in range(2):
                        if g == 1:
                            tensor.wait_ge(s_sc, strA(t, 0))
                        elif t > 0:
                            tensor.wait_ge(s_sc, strA(t - 1, 1))
                        for kk in range(8):
                            dc = g * 8 + kk
                            i = tensor.transpose(ps_tr[:, kk * 128 : (kk + 1) * 128],
                                                 xn_bf[:, dc * 128 : (dc + 1) * 128], eye_sb[:])
                        inc(i, "s_mm", s_mm, 1)
                    tensor.wait_ge(s_sc, strA(t, 1))
                    for j in range(4):
                        if j >= 2:
                            tensor.wait_ge(s_sc, skA(t, j - 2))
                        elif t > 0:
                            tensor.wait_ge(s_sc, svA(t - 1, 2 + (j % 2)))
                        for dc in range(DC):
                            tensor.matmul(ps_rot[j % 2][:], xnT_sb[:, dc * 128 : (dc + 1) * 128],
                                          wk_bf[:, dc, j * 512 : (j + 1) * 512],
                                          start=(dc == 0), stop=False)
                        i = tensor.matmul(ps_rot[j % 2][:], onesP_bf[:],
                                          bk_bf[:, j * 512 : (j + 1) * 512], start=False, stop=True)
                        inc(i, "s_mm", s_mm, 1)
                    for j in range(4):
                        if j >= 2:
                            tensor.wait_ge(s_sc, svA(t, j - 2))
                        else:
                            tensor.wait_ge(s_sc, skA(t, 2 + (j % 2)))
                        for dc in range(DC):
                            tensor.matmul(ps_rot[j % 2][:], xnT_sb[:, dc * 128 : (dc + 1) * 128],
                                          wv_bf[:, dc, j * 512 : (j + 1) * 512],
                                          start=(dc == 0), stop=False)
                        i = tensor.matmul(ps_rot[j % 2][:], onesP_bf[:],
                                          bv_bf[:, j * 512 : (j + 1) * 512], start=False, stop=True)
                        inc(i, "s_mm", s_mm, 1)
                    tensor.wait_ge(s_sc, svA(t, 3))
                    for h in range(H):
                        b, k4 = h // 4, h % 4
                        tensor.matmul(ps_kv[b][:, k4 * 128 : (k4 + 1) * 128],
                                      K_sb[:, h * 128 : (h + 1) * 128],
                                      V_sb[:, h * 128 : (h + 1) * 128],
                                      start=(t == 0), stop=(t == NT - 1), skip_group_check=True)
                    for h in range(H):
                        i = tensor.matmul(ps_ks[:, h : h + 1], K_sb[:, h * 128 : (h + 1) * 128],
                                          onesC_bf[:], start=(t == 0), stop=(t == NT - 1),
                                          skip_group_check=True)
                    inc(i, "s_mm", s_mm, 1)
                bdone(tensor)

            @blk.gpsimd
            def _(gpsimd):
                bwait(gpsimd)
                for k in GA:
                    g_front(gpsimd, k)
                bdone(gpsimd)

        BB[0] += 1
        esA.close()

        arena = es0.enter_context(nc.sbuf_tensor("arena", [128, 76800], BF16))
        xnT_own = arena[:, 0:16384].rearrange("p (a b) -> p a b", a=DC)
        esB1 = contextlib.ExitStack()
        xtb1_0 = esB1.enter_context(nc.sbuf_tensor("xtb1_0", [128, D], F32))
        xtb1 = [xtb1_0, xtb1_0]
        xnb1 = arena[:, 16384:18432]
        ps_trb = esB1.enter_context(nc.psum_tensor("ps_trb", [128, 1024], BF16))
        ps_acc1 = esB1.enter_context(nc.psum_tensor("ps_acc1", [128, 2048], F32))
        PS_ACC[0] = ps_acc1

        VPT1, SPT1, TPT1 = 6, 3, 2
        VB0, SB0, MB0 = C["s_ve"], C["s_sc"], C["s_mm"]
        XB1 = [C["s_x0"], C["s_x1"]]

        def vB1(t, k):
            return VB0 + t * VPT1 + k

        def sB1(t, k):
            return SB0 + t * SPT1 + k

        def mB1(t, k):
            return MB0 + t * TPT1 + k

        with nc.Block() as blk:
            @blk.sync
            def _(sync):
                bwait(sync)
                for t in range(OT):
                    if t >= 1:
                        sync.wait_ge(s_ve, vB1(t - 1, 4))
                    d = sync.dma_start(out=xtb1[0][:], in_=xo[t * 128 : (t + 1) * 128, :])
                    inc(d, "s_x0", s_xp[0], 16)
                bdone(sync)

            @blk.scalar
            def _(scalar):
                bwait(scalar)
                for t in range(OT):
                    scalar.wait_ge(s_ve, vB1(t, 1))
                    i = ln_scalar_pre(scalar, xtb1[0], t % 2)
                    inc(i, "s_sc", s_sc, 1)
                    scalar.wait_ge(s_ve, vB1(t, 2))
                    i = ln_scalar_mid(scalar, t % 2)
                    inc(i, "s_sc", s_sc, 1)
                    scalar.wait_ge(s_ve, vB1(t, 3))
                    i = ln_scalar_post(scalar, t % 2)
                    inc(i, "s_sc", s_sc, 1)
                bdone(scalar)

            @blk.vector
            def _(vector):
                bwait(vector)
                for t in range(OT):
                    vector.wait_ge(s_xp[0], XB1[0] + 16 * (t + 1))
                    i = ln_vec_sum(vector, xtb1[0], t % 2)
                    inc(i, "s_ve", s_ve, 1)
                    vector.wait_ge(s_sc, sB1(t, 1))
                    i = ln_vec_stats(vector, t % 2)
                    inc(i, "s_ve", s_ve, 1)
                    vector.wait_ge(s_sc, sB1(t, 2))
                    i = ln_vec_recip(vector, t % 2)
                    inc(i, "s_ve", s_ve, 1)
                    vector.wait_ge(s_sc, sB1(t, 3))
                    i = ln_vec_xn(vector, xtb1[0], xnb1, t % 2)
                    inc(i, "s_ve", s_ve, 1)
                    for g in range(2):
                        vector.wait_ge(s_mm, mB1(t, 1 + g))
                        dst = xnT_own[:, g * 8 : (g + 1) * 8, t * 128 : (t + 1) * 128]
                        i = vector.tensor_copy(dst, ps_trb[:].rearrange("p (a b) -> p a b", a=8))
                        inc(i, "s_ve", s_ve, 1)
                    for k in g_slice(GB1, t, OT):
                        g_vec_a(vector, k)
                bdone(vector)

            @blk.tensor
            def _(tensor):
                bwait(tensor)
                for t in range(OT):
                    tensor.wait_ge(s_ve, vB1(t, 4))
                    for g in range(2):
                        if g == 1:
                            tensor.wait_ge(s_ve, vB1(t, 5))
                        elif t > 0:
                            tensor.wait_ge(s_ve, vB1(t - 1, 6))
                        for kk in range(8):
                            dc = g * 8 + kk
                            i = tensor.transpose(ps_trb[:, kk * 128 : (kk + 1) * 128],
                                                 xnb1[:, dc * 128 : (dc + 1) * 128], eye_sb[:])
                        inc(i, "s_mm", s_mm, 1)
                    for k in g_slice(GB1, t, OT):
                        g_pe(tensor, k)
                bdone(tensor)

            @blk.gpsimd
            def _(gpsimd):
                bwait(gpsimd)
                for k in GB1:
                    g_front(gpsimd, k)
                bdone(gpsimd)

        BB[0] += 1
        esB1.close()

        Q_sb = arena[:, 16384:32768].rearrange("p (a b) -> p a b", a=H)
        esB2 = contextlib.ExitStack()
        xtb2_0 = esB2.enter_context(nc.sbuf_tensor("xtb2_0", [128, D], F32))
        xtb2 = [xtb2_0, xtb2_0]
        bq_bf = esB2.enter_context(nc.sbuf_tensor("bq_bf", [1, D], BF16))
        wbig = arena[:, 32768:65536].rearrange("p (a b) -> p a b", a=DC)
        ps_q = [esB2.enter_context(nc.psum_tensor(f"ps_q{i}", [128, 512], F32)) for i in range(4)]
        ps_acc2 = esB2.enter_context(nc.psum_tensor("ps_acc2", [128, 2048], F32))

        VB2, SB2, MB2 = C["s_ve"], C["s_sc"], C["s_mm"]
        XB2 = [C["s_x0"], C["s_x1"]]
        PS_ACC[0] = ps_acc2
        with nc.Block() as blk:
            @blk.sync
            def _(sync):
                bwait(sync)
                d = sync.dma_start(out=xtb2[0][0:1, :], in_=bq)
                inc(d, "s_x0", s_xp[0], 16)
                for dc in range(DC):
                    sync.wait_ge(s_sc, SB2 + dc + 1)
                    d = sync.dma_start(out=xtb2[0][:], in_=wqT[dc * 128 : (dc + 1) * 128, :])
                    inc(d, "s_x0", s_xp[0], 16)
                bdone(sync)

            @blk.scalar
            def _(scalar):
                bwait(scalar)
                scalar.wait_ge(s_xp[0], XB2[0] + 16)
                i = scalar.copy(bq_bf[:], xtb2[0][0:1, :])
                inc(i, "s_sc", s_sc, 1)
                for dc in range(DC):
                    scalar.wait_ge(s_xp[0], XB2[0] + 16 * (dc + 2))
                    i = scalar.copy(wbig[:, dc, :], xtb2[0][:])
                    inc(i, "s_sc", s_sc, 1)
                bdone(scalar)

            @blk.tensor
            def _(tensor):
                bwait(tensor)
                tensor.wait_ge(s_sc, SB2 + DC + 1)
                u = 0
                for qc in range(H):
                    for h2 in range(2):
                        if u >= 4:
                            tensor.wait_ge(s_ve, VB2 + (u - 4) + 1)
                        pq = ps_q[u % 4]
                        for dc in range(DC):
                            tensor.matmul(pq[:], wbig[:, dc, qc * 128 : (qc + 1) * 128],
                                          xnT_own[:, dc, h2 * 512 : (h2 + 1) * 512],
                                          start=(dc == 0), stop=False)
                        i = tensor.matmul(pq[:], bq_bf[:, qc * 128 : (qc + 1) * 128],
                                          ones_bf[:, 0:512], start=False, stop=True)
                        inc(i, "s_mm", s_mm, 1)
                        for k in g_slice(GB2, u, 32):
                            g_pe(tensor, k)
                        u += 1
                bdone(tensor)

            @blk.vector
            def _(vector):
                bwait(vector)
                u = 0
                for qc in range(H):
                    for h2 in range(2):
                        vector.wait_ge(s_mm, MB2 + u + 1)
                        i = vector.tensor_scalar(Q_sb[:, qc, h2 * 512 : (h2 + 1) * 512],
                                                 ps_q[u % 4][:], 0.0, None, ALU.max)
                        inc(i, "s_ve", s_ve, 1)
                        for k in g_slice(GB2, u, 32):
                            g_vec_a(vector, k)
                        u += 1
                bdone(vector)

            @blk.gpsimd
            def _(gpsimd):
                bwait(gpsimd)
                for k in GB2:
                    g_front(gpsimd, k)
                bdone(gpsimd)

        BB[0] += 1
        esB2.close()

        attnOT = arena[:, 32768:49152].rearrange("p (a b) -> p a b", a=DC)
        esB3 = contextlib.ExitStack()
        aO_sb = esB3.enter_context(nc.sbuf_tensor("aO_sb", [128, D], BF16))
        ps_z = esB3.enter_context(nc.psum_tensor("ps_z", [128, H], F32))
        ps_o = [esB3.enter_context(nc.psum_tensor(f"ps_o{i}", [128, 512], F32)) for i in range(2)]
        ps_tr3 = esB3.enter_context(nc.psum_tensor("ps_tr3", [128, 1024], BF16))
        ps_acc3 = esB3.enter_context(nc.psum_tensor("ps_acc3", [128, 2048], F32))
        PS_ACC[0] = ps_acc3

        VB3, SB3, MB3 = C["s_ve"], C["s_sc"], C["s_mm"]
        TPT3, VPT3, SPT3 = 7, 7, 2

        def mB3(t, k):
            return MB3 + t * TPT3 + k

        def vB3(t, k):
            return VB3 + t * VPT3 + k

        def sB3(t, k):
            return SB3 + t * SPT3 + k

        with nc.Block() as blk:
            @blk.tensor
            def _(tensor):
                bwait(tensor)
                for t in range(OT):
                    if t > 0:
                        tensor.wait_ge(s_sc, sB3(t - 1, 1))
                    for h in range(H):
                        i = tensor.matmul(ps_z[:, h : h + 1], Q_sb[:, h, t * 128 : (t + 1) * 128],
                                          Ksum_sb[:, h : h + 1], start=True, stop=True,
                                          skip_group_check=True)
                    inc(i, "s_mm", s_mm, 1)
                    for bb in range(4):
                        if bb >= 2:
                            tensor.wait_ge(s_ve, vB3(t, 2 + (bb - 2)))
                        elif t > 0:
                            tensor.wait_ge(s_ve, vB3(t - 1, 4 + bb))
                        for k4 in range(4):
                            h = bb * 4 + k4
                            i = tensor.matmul(ps_o[bb % 2][:, k4 * 128 : (k4 + 1) * 128],
                                              Q_sb[:, h, t * 128 : (t + 1) * 128],
                                              KV_sb[:, h * 128 : (h + 1) * 128],
                                              start=True, stop=True, skip_group_check=True)
                        inc(i, "s_mm", s_mm, 1)
                    tensor.wait_ge(s_ve, vB3(t, 5))
                    for g in range(2):
                        if g == 1:
                            tensor.wait_ge(s_ve, vB3(t, 6))
                        elif t > 0:
                            tensor.wait_ge(s_ve, vB3(t - 1, 7))
                        for kk in range(8):
                            mc = g * 8 + kk
                            i = tensor.transpose(ps_tr3[:, kk * 128 : (kk + 1) * 128],
                                                 aO_sb[:, mc * 128 : (mc + 1) * 128], eye_sb[:])
                        inc(i, "s_mm", s_mm, 1)
                    for k in g_slice(GB3, t, OT):
                        g_pe(tensor, k)
                bdone(tensor)

            @blk.scalar
            def _(scalar):
                bwait(scalar)
                for t in range(OT):
                    scalar.wait_ge(s_mm, mB3(t, 1))
                    i = scalar.activation(zt_sb[:], ps_z[:], AF.Relu, bias=epsat[:, 0:1],
                                          scale=dq_sb[:, t : t + 1])
                    inc(i, "s_sc", s_sc, 1)
                    scalar.wait_ge(s_ve, vB3(t, 1))
                    i = scalar.mul(zde_sb[:], zr_sb[:], dq_sb[:, t : t + 1])
                    inc(i, "s_sc", s_sc, 1)
                bdone(scalar)

            @blk.vector
            def _(vector):
                bwait(vector)
                for t in range(OT):
                    vector.wait_ge(s_sc, sB3(t, 1))
                    i = vector.reciprocal(zr_sb[:], zt_sb[:])
                    inc(i, "s_ve", s_ve, 1)
                    vector.wait_ge(s_sc, sB3(t, 2))
                    for bb in range(4):
                        vector.wait_ge(s_mm, mB3(t, 2 + bb))
                        for k4 in range(4):
                            h = bb * 4 + k4
                            vector.tensor_scalar(aO_sb[:, h * 128 : (h + 1) * 128],
                                                 ps_o[bb % 2][:, k4 * 128 : (k4 + 1) * 128],
                                                 zde_sb[:, h : h + 1], None, ALU.mult)
                        i = vector.drain()
                        inc(i, "s_ve", s_ve, 1)
                    for g in range(2):
                        vector.wait_ge(s_mm, mB3(t, 6 + g))
                        dst = attnOT[:, g * 8 : (g + 1) * 8, t * 128 : (t + 1) * 128]
                        i = vector.tensor_copy(dst, ps_tr3[:].rearrange("p (a b) -> p a b", a=8))
                        inc(i, "s_ve", s_ve, 1)
                    for k in g_slice(GB3, t, OT):
                        g_vec_a(vector, k)
                bdone(vector)

            @blk.gpsimd
            def _(gpsimd):
                bwait(gpsimd)
                for k in GB3:
                    g_front(gpsimd, k)
                bdone(gpsimd)

            @blk.sync
            def _(sync):
                bwait(sync)
                bdone(sync)

        BB[0] += 1
        esB3.close()

        woacc = arena[:, 49152:65536].rearrange("p (a b) -> p a b", a=OT)
        esB5a = contextlib.ExitStack()
        xtb5_0 = esB5a.enter_context(nc.sbuf_tensor("xtb5_0", [128, 1024], F32))
        xtb5 = [xtb5_0, xtb5_0]
        bo_bf = esB5a.enter_context(nc.sbuf_tensor("bo_bf", [1, D], BF16))
        wohalf = arena[:, 16384:32768].rearrange("p (a b) -> p a b", a=DC)
        ps_w = [esB5a.enter_context(nc.psum_tensor(f"ps_w{i}", [128, 1024], F32)) for i in range(2)]
        ps_acc5 = esB5a.enter_context(nc.psum_tensor("ps_acc5", [128, 2048], F32))
        PS_ACC[0] = ps_acc5

        VB5a, SB5a, MB5a = C["s_ve"], C["s_sc"], C["s_mm"]
        XB5a = [C["s_x0"], C["s_x1"]]
        with nc.Block() as blk:
            @blk.sync
            def _(sync):
                bwait(sync)
                d = sync.dma_start(out=xtb5[0][0:1, :], in_=bo[0:1, 0:1024])
                inc(d, "s_x0", s_xp[0], 16)
                sync.wait_ge(s_sc, SB5a + 1)
                d = sync.dma_start(out=xtb5[0][0:1, :], in_=bo[0:1, 1024:2048])
                inc(d, "s_x0", s_xp[0], 16)
                for hf in range(2):
                    for dc in range(DC):
                        j = hf * DC + dc
                        sync.wait_ge(s_sc, SB5a + j + 2)
                        d = sync.dma_start(out=xtb5[0][:],
                                           in_=woT[dc * 128 : (dc + 1) * 128,
                                                   hf * 1024 : (hf + 1) * 1024])
                        inc(d, "s_x0", s_xp[0], 16)
                bdone(sync)

            @blk.scalar
            def _(scalar):
                bwait(scalar)
                scalar.wait_ge(s_xp[0], XB5a[0] + 16)
                i = scalar.copy(bo_bf[0:1, 0:1024], xtb5[0][0:1, :])
                inc(i, "s_sc", s_sc, 1)
                scalar.wait_ge(s_xp[0], XB5a[0] + 32)
                i = scalar.copy(bo_bf[0:1, 1024:2048], xtb5[0][0:1, :])
                inc(i, "s_sc", s_sc, 1)
                for hf in range(2):
                    for dc in range(DC):
                        j = hf * DC + dc
                        scalar.wait_ge(s_xp[0], XB5a[0] + 16 * (j + 3))
                        if hf == 1:
                            scalar.wait_ge(s_mm, MB5a + 8)
                        i = scalar.copy(wohalf[:, dc, :], xtb5[0][:])
                        inc(i, "s_sc", s_sc, 1)
                bdone(scalar)

            @blk.tensor
            def _(tensor):
                bwait(tensor)
                u = 0
                for hf in range(2):
                    tensor.wait_ge(s_sc, SB5a + DC * (hf + 1) + 2)
                    for t in range(OT):
                        if u >= 2:
                            tensor.wait_ge(s_ve, VB5a + (u - 2) + 1)
                        pw = ps_w[u % 2]
                        for mc in range(DC):
                            for j2 in range(2):
                                tensor.matmul(pw[:, j2 * 512 : (j2 + 1) * 512],
                                              attnOT[:, mc, t * 128 : (t + 1) * 128],
                                              wohalf[:, mc, j2 * 512 : (j2 + 1) * 512],
                                              start=(mc == 0), stop=False)
                        for j2 in range(2):
                            i = tensor.matmul(pw[:, j2 * 512 : (j2 + 1) * 512], onesP_bf[:],
                                              bo_bf[:, hf * 1024 + j2 * 512 : hf * 1024 + (j2 + 1) * 512],
                                              start=False, stop=True)
                        inc(i, "s_mm", s_mm, 1)
                        for k in g_slice(GB5a, u, 16):
                            g_pe(tensor, k)
                        u += 1
                bdone(tensor)

            @blk.vector
            def _(vector):
                bwait(vector)
                u = 0
                for hf in range(2):
                    for t in range(OT):
                        for k in g_slice(GB5a, u, 16):
                            g_vec_a(vector, k)
                        vector.wait_ge(s_mm, MB5a + u + 1)
                        i = vector.tensor_copy(woacc[:, t, hf * 1024 : (hf + 1) * 1024], ps_w[u % 2][:])
                        inc(i, "s_ve", s_ve, 1)
                        u += 1
                bdone(vector)

            @blk.gpsimd
            def _(gpsimd):
                bwait(gpsimd)
                for k in GB5a:
                    g_front(gpsimd, k)
                bdone(gpsimd)

        BB[0] += 1
        esB5a.close()

        hT_a = arena[:, 16384:49152].rearrange("p (a b) -> p a b", a=32)
        hT_b = arena[:, 65536:76800].rearrange("p (a b) -> p a b", a=IC - 32)

        def hT_sl(ic, fsl):
            return hT_a[:, ic, fsl] if ic < 32 else hT_b[:, ic - 32, fsl]
        esB4 = contextlib.ExitStack()
        w1sb_ = esB4.enter_context(nc.sbuf_tensor("w1sb", [128, D], BF16))
        w2sb_ = esB4.enter_context(nc.sbuf_tensor("w2sb", [128, D], BF16))
        w1b = [w1sb_, KV_sb]
        w2b = [w2sb_, wt2]
        silu_sb = [esB4.enter_context(nc.sbuf_tensor(f"silu{i}", [128, 512], F32)) for i in range(2)]
        ps_g = [esB4.enter_context(nc.psum_tensor(f"ps_g{i}", [128, 1024], F32)) for i in range(2)]
        ps_u = [esB4.enter_context(nc.psum_tensor(f"ps_u{i}", [128, 1024], F32)) for i in range(2)]

        VB4, SB4, MB4 = C["s_ve"], C["s_sc"], C["s_mm"]
        XB4 = [C["s_x0"], C["s_x1"]]
        with nc.Block() as blk:
            @blk.sync
            def _(sync):
                bwait(sync)
                for ic in range(IC):
                    sync.wait_ge(s_go, 16 * (2 * ic + 2))
                    if ic >= 2:
                        sync.wait_ge(s_mm, MB4 + 4 * (ic - 1))
                    if ic >= 1:
                        sync.wait_ge(s_xp[0], XB4[0] + 16 * ic)
                        sync.wait_ge(s_xp[1], XB4[1] + 16 * ic)
                    d = sync.dma_start(out=w1b[ic % 2][:], in_=wh[0][ic, :, :])
                    inc(d, "s_x0", s_xp[0], 16)
                    d = sync.dma_start(out=w2b[ic % 2][:], in_=wh[1][ic, :, :])
                    inc(d, "s_x1", s_xp[1], 16)
                bdone(sync)

            @blk.tensor
            def _(tensor):
                bwait(tensor)
                for ic in range(IC):
                    tensor.wait_ge(s_xp[0], XB4[0] + 16 * (ic + 1))
                    tensor.wait_ge(s_xp[1], XB4[1] + 16 * (ic + 1))
                    for h2 in range(2):
                        if ic >= 2:
                            tensor.wait_ge(s_ve, VB4 + 2 * (ic - 2) + h2 + 1)
                        pg, pu = ps_g[ic % 2], ps_u[ic % 2]
                        for dc in range(DC):
                            i = tensor.matmul(pg[:, h2 * 512 : (h2 + 1) * 512],
                                              w1b[ic % 2][:, dc * 128 : (dc + 1) * 128],
                                              xnT_own[:, dc, h2 * 512 : (h2 + 1) * 512],
                                              start=(dc == 0), stop=(dc == DC - 1))
                        inc(i, "s_mm", s_mm, 1)
                        for dc in range(DC):
                            i = tensor.matmul(pu[:, h2 * 512 : (h2 + 1) * 512],
                                              w2b[ic % 2][:, dc * 128 : (dc + 1) * 128],
                                              xnT_own[:, dc, h2 * 512 : (h2 + 1) * 512],
                                              start=(dc == 0), stop=(dc == DC - 1))
                        inc(i, "s_mm", s_mm, 1)
                bdone(tensor)

            @blk.scalar
            def _(scalar):
                bwait(scalar)
                for ic in range(IC):
                    for h2 in range(2):
                        scalar.wait_ge(s_mm, MB4 + 4 * ic + 2 * h2 + 1)
                        if ic > 0:
                            scalar.wait_ge(s_ve, VB4 + 2 * (ic - 1) + h2 + 1)
                        i = scalar.activation(silu_sb[h2][:], ps_g[ic % 2][:, h2 * 512 : (h2 + 1) * 512],
                                              AF.Silu)
                        inc(i, "s_sc", s_sc, 1)
                bdone(scalar)

            @blk.vector
            def _(vector):
                bwait(vector)
                for ic in range(IC):
                    for h2 in range(2):
                        vector.wait_ge(s_sc, SB4 + 2 * ic + h2 + 1)
                        vector.wait_ge(s_mm, MB4 + 4 * ic + 2 * h2 + 2)
                        i = vector.tensor_tensor(hT_sl(ic, slice(h2 * 512, (h2 + 1) * 512)),
                                                 silu_sb[h2][:],
                                                 ps_u[ic % 2][:, h2 * 512 : (h2 + 1) * 512],
                                                 ALU.mult)
                        inc(i, "s_ve", s_ve, 1)
                    for k in g_slice(GB4, ic, IC):
                        g_vec(vector, k)
                bdone(vector)

            @blk.gpsimd
            def _(gpsimd):
                bwait(gpsimd)
                for k in GB4:
                    g_front(gpsimd, k)
                bdone(gpsimd)

        BB[0] += 1
        esB4.close()

        esB5b = contextlib.ExitStack()
        NW3 = 3
        w3qb = [arena[:, i * 512 : (i + 1) * 512] for i in range(NW3)]
        xo8 = [esB5b.enter_context(nc.sbuf_tensor(f"xo8_{i}", [128, 512], F32)) for i in range(8)]
        ps8 = [esB5b.enter_context(nc.psum_tensor(f"ps8_{i}", [128, 512], F32)) for i in range(8)]

        VB5, SB5, MB5 = C["s_ve"], C["s_sc"], C["s_mm"]
        XB5, XO0 = C["s_x0"], C["s_xo"]

        def fB5(j):
            return MB5 + (j // IC) * (IC + 1) + 2 + (j % IC)

        with nc.Block() as blk:
            @blk.scalar
            def _(scalar):
                bwait(scalar)
                bdone(scalar)

            @blk.sync
            def _(sync):
                bwait(sync)
                for q in range(4):
                    for ic in range(IC):
                        j = IC * q + ic
                        sync.wait_ge(s_go, 16 * (2 * IC + ic + 1))
                        if j >= NW3:
                            sync.wait_ge(s_mm, fB5(j - NW3))
                        if j >= 1:
                            sync.wait_ge(s_xp[0], XB5 + 16 * j)
                        d = sync.dma_start(out=w3qb[j % NW3],
                                           in_=wh[2][ic, :, q * 512 : (q + 1) * 512])
                        inc(d, "s_x0", s_xp[0], 16)
                    for t in range(8):
                        m = 8 * q + t
                        if q > 0:
                            n = (8 * (q - 1) + t - t % 2) // 2 + 1
                            sync.wait_ge(s_oa[t % 2], 16 * n)
                        if m >= 1:
                            sync.wait_ge(s_xo, XO0 + 16 * m)
                        d = sync.dma_start(out=xo8[t][:],
                                           in_=xo[t * 128 : (t + 1) * 128, q * 512 : (q + 1) * 512])
                        inc(d, "s_xo", s_xo, 16)
                bdone(sync)

            @blk.tensor
            def _(tensor):
                bwait(tensor)
                for q in range(4):
                    if q > 0:
                        tensor.wait_ge(s_ve, VB5 + 8 * q)
                    for t in range(8):
                        i = tensor.matmul(ps8[t][:], eye_sb[:],
                                          woacc[:, t, q * 512 : (q + 1) * 512],
                                          start=True, stop=False, skip_group_check=True)
                    inc(i, "s_mm", s_mm, 1)
                    for ic in range(IC):
                        j = IC * q + ic
                        tensor.wait_ge(s_xp[0], XB5 + 16 * (j + 1))
                        for t in range(8):
                            i = tensor.matmul(ps8[t][:],
                                              hT_sl(ic, slice(t * 128, (t + 1) * 128)),
                                              w3qb[j % NW3], start=False, stop=(ic == IC - 1),
                                              skip_group_check=True)
                        inc(i, "s_mm", s_mm, 1)
                bdone(tensor)

            @blk.vector
            def _(vector):
                bwait(vector)
                for q in range(4):
                    vector.wait_ge(s_mm, MB5 + (IC + 1) * (q + 1))
                    for t in range(8):
                        vector.wait_ge(s_xo, XO0 + 16 * (8 * q + t + 1))
                        i = vector.tensor_tensor(xo8[t][:], ps8[t][:], xo8[t][:], ALU.add)
                        inc(i, "s_ve", s_ve, 1)
                bdone(vector)

            @blk.gpsimd
            def _(gpsimd):
                bwait(gpsimd)
                for q in range(4):
                    for t in range(8):
                        gpsimd.wait_ge(s_ve, VB5 + 8 * q + t + 1)
                        n = (8 * q + t - t % 2) // 2
                        if n > 0:
                            gpsimd.wait_ge(s_oa[t % 2], 16 * n)
                        d = gpsimd.dma_start(out=y[t * 128 : (t + 1) * 128, q * 512 : (q + 1) * 512],
                                             in_=xo8[t][:])
                        inc(d, f"s_out{t % 2}", s_oa[t % 2], 16)
                gpsimd.wait_ge(s_oa[0], 16 * 16)
                gpsimd.wait_ge(s_oa[1], 16 * 16)
                bdone(gpsimd)

        BB[0] += 1
        esB5b.close()

    return nc


_NC_CACHE = [None]


def kernel(L0_attn_x, L0_attn_Wq, L0_attn_bq, L0_attn_Wk, L0_attn_bk,
           L0_attn_Wv, L0_attn_bv, L0_attn_Wo, L0_attn_bo,
           L0_ffn_lut, L0_ffn_gate_idx, L0_ffn_up_idx, L0_ffn_down_idx):
    x = np.ascontiguousarray(np.asarray(L0_attn_x, np.float32))

    def f32(a):
        return np.ascontiguousarray(np.asarray(a, np.float32))

    wqT = f32(np.asarray(L0_attn_Wq).T)
    wkT = f32(np.asarray(L0_attn_Wk).T)
    wvT = f32(np.asarray(L0_attn_Wv).T)
    woT = f32(np.asarray(L0_attn_Wo).T)
    bq_ = f32(L0_attn_bq).reshape(1, D)
    bk_ = f32(L0_attn_bk).reshape(1, D)
    bv_ = f32(L0_attn_bv).reshape(1, D)
    bo_ = f32(L0_attn_bo).reshape(1, D)
    lut = f32(L0_ffn_lut).reshape(1, 16)

    gidx = np.asarray(L0_ffn_gate_idx, np.int32)
    uidx = np.asarray(L0_ffn_up_idx, np.int32)
    didx = np.asarray(L0_ffn_down_idx, np.int32)
    g1 = np.ascontiguousarray(
        gidx.astype(np.int8).reshape(IC, 128, DC, 128).transpose(0, 3, 2, 1)).reshape(IC, 128, D)
    g2 = np.ascontiguousarray(
        uidx.astype(np.int8).reshape(IC, 128, DC, 128).transpose(0, 3, 2, 1)).reshape(IC, 128, D)
    d3 = np.ascontiguousarray(didx.astype(np.int8).T).reshape(IC, 128, D)

    pos = np.arange(S, dtype=np.float64)
    dec = np.exp(-pos / S).astype(np.float32)
    dk_cols = np.ascontiguousarray(dec.reshape(NT, 128).T)
    eye = np.eye(128).astype(ml_dtypes.bfloat16)

    if _NC_CACHE[0] is None:
        _NC_CACHE[0] = build()
    nc = _NC_CACHE[0]

    in_maps = []
    for c in range(8):
        b, q = c // 4, c % 4
        in_maps.append({
            "xb": np.ascontiguousarray(x[b]),
            "xo": np.ascontiguousarray(x[b, q * T : (q + 1) * T]),
            "wkT": wkT, "wvT": wvT, "wqT": wqT, "woT": woT,
            "bk": bk_, "bv": bv_, "bq": bq_, "bo": bo_,
            "g1": g1, "g2": g2, "d3": d3,
            "lut16": lut,
            "dk_cols": dk_cols,
            "dq_cols": np.ascontiguousarray(dk_cols[:, q * OT : (q + 1) * OT]),
            "eye_in": eye,
        })
    res = run_bass_kernel_spmd(nc, in_maps, core_ids=list(range(8)))
    out = np.empty((2, S, D), np.float32)
    for c in range(8):
        b, q = c // 4, c % 4
        out[b, q * T : (q + 1) * T] = np.asarray(res.results[c]["y"], np.float32)
    return out


# version 16
# speedup vs baseline: 1.1327x; 1.0346x over previous
import contextlib

import numpy as np
import ml_dtypes

import concourse.bass as bass
import concourse.mybir as mybir
from concourse.bass_utils import run_bass_kernel_spmd

F32 = mybir.dt.float32
BF16 = mybir.dt.bfloat16
I8 = mybir.dt.int8
AF = mybir.ActivationFunctionType
ALU = mybir.AluOpType
AX = mybir.AxisListType.X

D = 2048
H = 16
S = 4096
T = 1024
NT = 32
OT = 8
DC = 16
IC = 43
NITER = 3 * IC


def g_mat_ic(k):
    if k < 2 * IC:
        return (k % 2, k // 2)
    return (2, k - 2 * IC)


def build():
    nc = bass.Bass("TRN2", target_bir_lowering=False)

    xb = nc.dram_tensor("xb", [S, D], F32, kind="ExternalInput").ap()
    xo = nc.dram_tensor("xo", [T, D], F32, kind="ExternalInput").ap()
    wkT = nc.dram_tensor("wkT", [D, D], F32, kind="ExternalInput").ap()
    wvT = nc.dram_tensor("wvT", [D, D], F32, kind="ExternalInput").ap()
    wqT = nc.dram_tensor("wqT", [D, D], F32, kind="ExternalInput").ap()
    woT = nc.dram_tensor("woT", [D, D], F32, kind="ExternalInput").ap()
    bk = nc.dram_tensor("bk", [1, D], F32, kind="ExternalInput").ap()
    bv = nc.dram_tensor("bv", [1, D], F32, kind="ExternalInput").ap()
    bq = nc.dram_tensor("bq", [1, D], F32, kind="ExternalInput").ap()
    bo = nc.dram_tensor("bo", [1, D], F32, kind="ExternalInput").ap()
    g1 = nc.dram_tensor("g1", [IC, 128, D], I8, kind="ExternalInput").ap()
    g2 = nc.dram_tensor("g2", [IC, 128, D], I8, kind="ExternalInput").ap()
    d3 = nc.dram_tensor("d3", [IC, 128, D], I8, kind="ExternalInput").ap()
    lut16 = nc.dram_tensor("lut16", [1, 16], F32, kind="ExternalInput").ap()
    dk_cols = nc.dram_tensor("dk_cols", [128, NT], F32, kind="ExternalInput").ap()
    dq_cols = nc.dram_tensor("dq_cols", [128, OT], F32, kind="ExternalInput").ap()
    eye_in = nc.dram_tensor("eye_in", [128, 128], BF16, kind="ExternalInput").ap()
    y = nc.dram_tensor("y", [T, D], F32, kind="ExternalOutput").ap()
    wh = [nc.dram_tensor(f"w{m}h", [IC, 128, D], BF16, kind="Internal").ap() for m in range(3)]
    gsrc = [g1, g2, d3]

    C = {}

    def mk_sem(es, name):
        h = es.enter_context(nc.semaphore(name))
        C[name] = 0
        return h

    def inc(instr, name, h, n):
        C[name] += n
        instr.then_inc(h, n)
        return C[name]

    BB = [0]

    es0 = contextlib.ExitStack()
    with es0:
        ones_bf = es0.enter_context(nc.sbuf_tensor("ones_bf", [1, 1024], BF16))
        onesP_bf = es0.enter_context(nc.sbuf_tensor("onesP_bf", [1, 128], BF16))
        onesC_bf = es0.enter_context(nc.sbuf_tensor("onesC_bf", [128, 1], BF16))
        eye_sb = es0.enter_context(nc.sbuf_tensor("eye_sb", [128, 128], BF16))
        lut_s = es0.enter_context(nc.sbuf_tensor("lut_s", [1, 16], F32))
        lut_b = es0.enter_context(nc.sbuf_tensor("lut_b", [1, 16], BF16))
        lutb = es0.enter_context(nc.sbuf_tensor("lutb", [128, 16], F32))
        dk_sb = es0.enter_context(nc.sbuf_tensor("dk_sb", [128, NT], F32))
        dq_sb = es0.enter_context(nc.sbuf_tensor("dq_sb", [128, OT], F32))
        epsln = es0.enter_context(nc.sbuf_tensor("epsln", [128, 1], F32))
        epsat = es0.enter_context(nc.sbuf_tensor("epsat", [128, 1], F32))
        st = es0.enter_context(nc.sbuf_tensor("st", [128, 32], F32))
        gi = [es0.enter_context(nc.sbuf_tensor(f"gi{i}", [128, D], I8)) for i in range(2)]
        gbs = [es0.enter_context(nc.sbuf_tensor(f"gb{i}", [128, D], BF16)) for i in range(2)]
        wts = [es0.enter_context(nc.sbuf_tensor(f"wt{i}", [128, D], BF16)) for i in range(2)]
        planes = [es0.enter_context(nc.sbuf_tensor(f"plane{i}", [128, D], BF16)) for i in range(4)]
        wt2 = es0.enter_context(nc.sbuf_tensor("wt2", [128, D], BF16))

        KV_sb = es0.enter_context(nc.sbuf_tensor("KV_sb", [128, H * 128], BF16))
        Ksum_sb = es0.enter_context(nc.sbuf_tensor("Ksum_sb", [128, H], BF16))
        zt_sb = es0.enter_context(nc.sbuf_tensor("zt_sb", [128, H], F32))
        zr_sb = es0.enter_context(nc.sbuf_tensor("zr_sb", [128, H], F32))
        zde_sb = es0.enter_context(nc.sbuf_tensor("zde_sb", [128, H], F32))

        s_misc = mk_sem(es0, "s_misc")
        s_xp = [mk_sem(es0, "s_x0"), mk_sem(es0, "s_x1")]
        s_xo = mk_sem(es0, "s_xo")
        s_ip = [mk_sem(es0, "s_i0"), mk_sem(es0, "s_i1")]
        s_mm = mk_sem(es0, "s_mm")
        s_ve = mk_sem(es0, "s_ve")
        s_sc = mk_sem(es0, "s_sc")
        s_gp = mk_sem(es0, "s_gp")
        s_gc = mk_sem(es0, "s_gc")
        s_gv = mk_sem(es0, "s_gv")
        s_vv = mk_sem(es0, "s_vv")
        s_go = mk_sem(es0, "s_go")
        s_oa = [mk_sem(es0, "s_out0"), mk_sem(es0, "s_out1")]
        s_ph = mk_sem(es0, "s_ph")
        s_pl = mk_sem(es0, "s_pl")
        s_pm = mk_sem(es0, "s_pm")

        def bwait(e):
            if BB[0] > 0:
                e.wait_ge(s_ph, 5 * BB[0])

        def bdone(e):
            e.sem_inc(s_ph, 1)

        def g_dma_idx(eng, k):
            m, ic = g_mat_ic(k)
            if k >= 2:
                eng.wait_ge(s_gc, VEC_IDX[k - 2] + 1)
            d = eng.dma_start(out=gi[k % 2][:], in_=gsrc[m][ic, :, :])
            inc(d, f"s_i{k % 2}", s_ip[k % 2], 16)

        def g_cast(eng, k):
            vi = VEC_IDX[k]
            eng.wait_ge(s_ip[k % 2], 16 * (k // 2 + 1))
            if k >= 2:
                eng.wait_ge(s_gv, VEC_IDX[k - 2] + 1)
            i = eng.tensor_copy(gbs[k % 2][:], gi[k % 2][:])
            inc(i, "s_gc", s_gc, 1)

        def g_front(gpsimd, k):
            if k + 1 < NITER:
                g_dma_idx(gpsimd, k + 1)
            g_cast(gpsimd, k)
            if k > 0:
                g_out(gpsimd, k - 1)
            if k == NITER - 1:
                g_out(gpsimd, k)

        def g_vec(vector, k):
            if k in GP_OWN:
                return
            vi = VEC_IDX[k]
            gb = gbs[k % 2]
            wt = wts[k % 2]
            vector.wait_ge(s_gc, vi + 1)
            if ASSIST_LIST and k > ASSIST_LIST[-1]:
                vector.wait_ge(s_pm, 16 * len(ASSIST_LIST))
            if k > 1:
                vector.wait_ge(s_go, 16 * (k - 1))
            vector.tensor_scalar(wt[:], gb[:], 0.0, lutb[:, 0:1], ALU.is_equal, ALU.mult)
            for v in range(1, 16):
                vector.tensor_scalar(planes[v % 2][:], gb[:], float(v), lutb[:, v : v + 1],
                                     ALU.is_equal, ALU.mult)
                vector.drain()
                vector.tensor_tensor(wt[:], wt[:], planes[v % 2][:], ALU.add)
            i = vector.drain()
            inc(i, "s_gv", s_gv, 1)

        def g_out(gpsimd, k):
            m, ic = g_mat_ic(k)
            gpsimd.wait_ge(s_gv, VEC_IDX[k] + 1)
            if k > 0:
                gpsimd.wait_ge(s_go, 16 * k)
            d = gpsimd.dma_start(out=wh[m][ic, :, :], in_=wts[k % 2][:])
            inc(d, "s_go", s_go, 16)

        GP_OWN = set()
        VEC_IDX = {}
        _vi = 0
        for _k in range(NITER):
            if _k not in GP_OWN:
                VEC_IDX[_k] = _vi
                _vi += 1
        NVEC = _vi

        GA = list(range(0, 30))
        GB1 = list(range(30, 40))
        GB2 = list(range(40, 62))
        GB3 = list(range(62, 84))
        GB5a = list(range(84, 109))
        GB4 = list(range(109, NITER))

        def g_slice(lst, i, n):
            return lst[len(lst) * i // n : len(lst) * (i + 1) // n]

        ASSIST_LIST = GB1 + GB2 + GB3 + GB5a
        AORD = {k: a for a, k in enumerate(ASSIST_LIST)}
        PS_ACC = [None]

        def g_vec_a(vector, k):
            a = AORD[k]
            gb = gbs[k % 2]
            vector.wait_ge(s_gc, VEC_IDX[k] + 1)
            for v in range(16):
                p = a * 16 + v
                if p >= 4:
                    vector.wait_ge(s_pm, p - 3)
                i = vector.tensor_scalar(planes[v % 4][:], gb[:], float(v),
                                         lutb[:, v : v + 1], ALU.is_equal, ALU.mult)
                inc(i, "s_pl", s_pl, 1)
            vector.wait_ge(s_pm, (a + 1) * 16)
            if k > 1:
                vector.wait_ge(s_go, 16 * (k - 1))
            i = vector.tensor_copy(wts[k % 2][:], PS_ACC[0][:])
            inc(i, "s_gv", s_gv, 1)

        def g_pe(tensor, k):
            a = AORD[k]
            ps_acc = PS_ACC[0]
            for v in range(16):
                tensor.wait_ge(s_pl, a * 16 + v + 1)
                if v == 0:
                    tensor.wait_ge(s_gv, k)
                for j4 in range(4):
                    i = tensor.matmul(ps_acc[:, j4 * 512 : (j4 + 1) * 512], eye_sb[:],
                                      planes[v % 4][:, j4 * 512 : (j4 + 1) * 512],
                                      start=(v == 0), stop=(v == 15), skip_group_check=True)
                inc(i, "s_pm", s_pm, 1)

        def ln_vec_sum(vector, x_t, pb):
            b = 16 * pb
            return vector.reduce_sum(st[:, b : b + 1], x_t[:], axis=AX)

        def ln_scalar_pre(scalar, x_t, pb):
            b = 16 * pb
            scalar.activation(wt2[:], x_t[:], AF.Square, accum_out=st[:, b + 3 : b + 4])
            scalar.mul(st[:, b + 1 : b + 2], st[:, b : b + 1], -1.0 / D)
            return scalar.drain()

        def ln_vec_stats(vector, pb):
            b = 16 * pb
            vector.tensor_tensor(st[:, b + 4 : b + 5], st[:, b : b + 1], st[:, b : b + 1], ALU.mult)
            vector.drain()
            return vector.scalar_tensor_tensor(st[:, b + 5 : b + 6], st[:, b + 3 : b + 4], float(D),
                                               st[:, b + 4 : b + 5], ALU.mult, ALU.subtract)

        def ln_scalar_mid(scalar, pb):
            b = 16 * pb
            return scalar.activation(st[:, b + 6 : b + 7], st[:, b + 5 : b + 6], AF.Sqrt,
                                     bias=epsln[:, 0:1], scale=1.0 / (float(D) * float(D)))

        def ln_vec_recip(vector, pb):
            b = 16 * pb
            return vector.reciprocal(st[:, b + 7 : b + 8], st[:, b + 6 : b + 7])

        def ln_scalar_post(scalar, pb):
            b = 16 * pb
            return scalar.copy(st[:, b + 8 : b + 9], st[:, b + 7 : b + 8])

        def ln_vec_xn(vector, x_t, xn_out, pb):
            b = 16 * pb
            return vector.tensor_scalar(xn_out[:], x_t[:], st[:, b + 1 : b + 2],
                                        st[:, b + 8 : b + 9], ALU.add, ALU.mult)

        esA = contextlib.ExitStack()
        xt = [esA.enter_context(nc.sbuf_tensor(f"xt{i}", [128, D], F32)) for i in range(2)]
        xn_bf = esA.enter_context(nc.sbuf_tensor("xn_bf", [128, D], BF16))
        xnT_sb = esA.enter_context(nc.sbuf_tensor("xnT_sb", [128, D], BF16))
        K_sb = esA.enter_context(nc.sbuf_tensor("K_sb", [128, D], BF16))
        V_sb = esA.enter_context(nc.sbuf_tensor("V_sb", [128, D], BF16))
        wk_bf = esA.enter_context(nc.sbuf_tensor("wk_bf", [128, DC, D], BF16))
        wv_bf = esA.enter_context(nc.sbuf_tensor("wv_bf", [128, DC, D], BF16))
        ps_kv = [esA.enter_context(nc.psum_tensor(f"ps_kv{b}", [128, 512], F32)) for b in range(4)]
        ps_ks = esA.enter_context(nc.psum_tensor("ps_ks", [128, H], F32))
        ps_rot = [esA.enter_context(nc.psum_tensor(f"ps_rot{b}", [128, 512], F32)) for b in range(2)]
        ps_tr = esA.enter_context(nc.psum_tensor("ps_tr", [128, 1024], BF16))
        bk_bf = planes[2][0:1, 0:D]
        bv_bf = planes[3][0:1, 0:D]

        with nc.Block() as blk:
            @blk.sync
            def _(sync):
                sync.dma_start(out=lut_s[:], in_=lut16).then_inc(s_misc, 16)
                sync.dma_start(out=dk_sb[:], in_=dk_cols).then_inc(s_misc, 16)
                sync.dma_start(out=dq_sb[:], in_=dq_cols).then_inc(s_misc, 16)
                sync.dma_start(out=eye_sb[:], in_=eye_in).then_inc(s_misc, 16)
                sync.dma_start(out=xt[0][0:1, 0:D], in_=bk).then_inc(s_misc, 16)
                sync.dma_start(out=xt[1][0:1, 0:D], in_=bv).then_inc(s_misc, 16)
                C["s_misc"] += 6 * 16
                for j in range(32):
                    src = wkT if j < 16 else wvT
                    dc = j % 16
                    if j < 2:
                        sync.wait_ge(s_sc, 2)
                    else:
                        sync.wait_ge(s_sc, 3 + (j - 2) + 1)
                    d = sync.dma_start(out=xt[j % 2][:], in_=src[dc * 128 : (dc + 1) * 128, :])
                    inc(d, f"s_x{j % 2}", s_xp[j % 2], 16)
                bdone(sync)

            @blk.gpsimd
            def _(gpsimd):
                gpsimd.memset(ones_bf[:], 1.0)
                gpsimd.memset(onesP_bf[:], 1.0)
                gpsimd.memset(onesC_bf[:], 1.0)
                gpsimd.memset(epsln[:], 1e-5)
                gpsimd.memset(epsat[:], 1e-6)
                i = gpsimd.drain()
                inc(i, "s_gp", s_gp, 1)
                g_dma_idx(gpsimd, 0)
                bdone(gpsimd)

            @blk.scalar
            def _(scalar):
                scalar.wait_ge(s_misc, 6 * 16)
                scalar.copy(bk_bf[:], xt[0][0:1, 0:D]).then_inc(s_sc, 1)
                scalar.copy(bv_bf[:], xt[1][0:1, 0:D]).then_inc(s_sc, 1)
                scalar.copy(lut_b[:], lut_s[:]).then_inc(s_sc, 1)
                C["s_sc"] += 3
                for j in range(32):
                    src = wk_bf if j < 16 else wv_bf
                    dc = j % 16
                    scalar.wait_ge(s_xp[j % 2], 16 * (j // 2 + 1))
                    i = scalar.copy(src[:, dc, :], xt[j % 2][:])
                    inc(i, "s_sc", s_sc, 1)
                bdone(scalar)

            @blk.tensor
            def _(tensor):
                tensor.wait_ge(s_sc, 5)
                tensor.wait_ge(s_gp, 1)
                i = tensor.matmul(ps_ks[:], onesP_bf[:], lut_b[:], start=True, stop=True)
                inc(i, "s_mm", s_mm, 1)
                bdone(tensor)

            @blk.vector
            def _(vector):
                vector.wait_ge(s_mm, 1)
                i = vector.tensor_copy(lutb[:], ps_ks[:])
                inc(i, "s_ve", s_ve, 1)
                bdone(vector)

        BB[0] += 1

        TPT = 11
        VA0, SA0, MA0 = C["s_ve"], C["s_sc"], C["s_mm"]
        XA = [C["s_x0"], C["s_x1"]]

        EVv, EVs = {}, {}
        _cv = VA0
        for _t in range(NT + 1):
            if _t < NT:
                _cv += 1; EVv[("sum", _t)] = _cv
            _u = _t - 1
            if 0 <= _u < NT:
                _cv += 1; EVv[("st", _u)] = _cv
                _cv += 1; EVv[("re", _u)] = _cv
                _cv += 1; EVv[("xn", _u)] = _cv
        _cs = SA0
        for _t in range(NT + 2):
            if _t < NT:
                _cs += 1; EVs[("pre", _t)] = _cs
            _u = _t - 1
            if 0 <= _u < NT:
                _cs += 1; EVs[("mid", _u)] = _cs
                _cs += 1; EVs[("post", _u)] = _cs
            _w = _t - 2
            if 0 <= _w < NT:
                for _g in range(2):
                    _cs += 1; EVs[("tr", _w, _g)] = _cs
                for _j in range(4):
                    _cs += 1; EVs[("k", _w, _j)] = _cs
                for _j in range(4):
                    _cs += 1; EVs[("v", _w, _j)] = _cs

        def vsumA(t):
            return EVv[("sum", t)]

        def vstA(u):
            return EVv[("st", u)]

        def vreA(u):
            return EVv[("re", u)]

        def vxnA(u):
            return EVv[("xn", u)]

        def spreA(t):
            return EVs[("pre", t)]

        def smidA(u):
            return EVs[("mid", u)]

        def spostA(u):
            return EVs[("post", u)]

        def strA(w, g):
            return EVs[("tr", w, g)]

        def skA(w, j):
            return EVs[("k", w, j)]

        def svA(w, j):
            return EVs[("v", w, j)]

        def mA(t, k):
            return MA0 + t * TPT + k

        with nc.Block() as blk:
            @blk.sync
            def _(sync):
                bwait(sync)
                for t in range(NT):
                    if t >= 2:
                        sync.wait_ge(s_ve, vxnA(t - 2))
                    d = sync.dma_start(out=xt[t % 2][:], in_=xb[t * 128 : (t + 1) * 128, :])
                    inc(d, f"s_x{t % 2}", s_xp[t % 2], 16)
                bdone(sync)

            @blk.vector
            def _(vector):
                bwait(vector)
                for t in range(NT + 1):
                    if t < NT:
                        vector.wait_ge(s_xp[t % 2], XA[t % 2] + 16 * (t // 2 + 1))
                        i = ln_vec_sum(vector, xt[t % 2], t % 2)
                        inc(i, "s_ve", s_ve, 1)
                        assert C["s_ve"] == vsumA(t)
                    u = t - 1
                    if 0 <= u < NT:
                        vector.wait_ge(s_sc, spreA(u))
                        i = ln_vec_stats(vector, u % 2)
                        inc(i, "s_ve", s_ve, 1)
                        vector.wait_ge(s_sc, smidA(u))
                        i = ln_vec_recip(vector, u % 2)
                        inc(i, "s_ve", s_ve, 1)
                        vector.wait_ge(s_sc, spostA(u))
                        if u > 0:
                            vector.wait_ge(s_mm, mA(u - 1, 2))
                        i = ln_vec_xn(vector, xt[u % 2], xn_bf, u % 2)
                        inc(i, "s_ve", s_ve, 1)
                        assert C["s_ve"] == vxnA(u)
                        for k in g_slice(GA, u, NT):
                            g_vec(vector, k)
                vector.wait_ge(s_mm, mA(NT - 1, 11))
                for b in range(4):
                    i = vector.tensor_copy(KV_sb[:, b * 512 : (b + 1) * 512], ps_kv[b][:])
                    inc(i, "s_ve", s_ve, 1)
                i = vector.tensor_copy(Ksum_sb[:], ps_ks[:])
                inc(i, "s_ve", s_ve, 1)
                bdone(vector)

            @blk.scalar
            def _(scalar):
                bwait(scalar)
                for t in range(NT + 2):
                    if t < NT:
                        scalar.wait_ge(s_ve, vsumA(t))
                        i = ln_scalar_pre(scalar, xt[t % 2], t % 2)
                        inc(i, "s_sc", s_sc, 1)
                        assert C["s_sc"] == spreA(t)
                    u = t - 1
                    if 0 <= u < NT:
                        scalar.wait_ge(s_ve, vstA(u))
                        i = ln_scalar_mid(scalar, u % 2)
                        inc(i, "s_sc", s_sc, 1)
                        scalar.wait_ge(s_ve, vreA(u))
                        i = ln_scalar_post(scalar, u % 2)
                        inc(i, "s_sc", s_sc, 1)
                    w = t - 2
                    if 0 <= w < NT:
                        for g in range(2):
                            scalar.wait_ge(s_mm, mA(w, 1 + g))
                            i = scalar.copy(xnT_sb[:, g * 1024 : (g + 1) * 1024], ps_tr[:])
                            inc(i, "s_sc", s_sc, 1)
                        for j in range(4):
                            scalar.wait_ge(s_mm, mA(w, 3 + j))
                            i = scalar.activation(K_sb[:, j * 512 : (j + 1) * 512], ps_rot[j % 2][:],
                                                  AF.Relu, scale=dk_sb[:, w : w + 1])
                            inc(i, "s_sc", s_sc, 1)
                        for j in range(4):
                            scalar.wait_ge(s_mm, mA(w, 7 + j))
                            i = scalar.copy(V_sb[:, j * 512 : (j + 1) * 512], ps_rot[j % 2][:])
                            inc(i, "s_sc", s_sc, 1)
                        assert C["s_sc"] == svA(w, 3)
                bdone(scalar)

            @blk.tensor
            def _(tensor):
                bwait(tensor)
                for t in range(NT):
                    tensor.wait_ge(s_ve, vxnA(t))
                    for g in range(2):
                        if g == 1:
                            tensor.wait_ge(s_sc, strA(t, 0))
                        elif t > 0:
                            tensor.wait_ge(s_sc, strA(t - 1, 1))
                        for kk in range(8):
                            dc = g * 8 + kk
                            i = tensor.transpose(ps_tr[:, kk * 128 : (kk + 1) * 128],
                                                 xn_bf[:, dc * 128 : (dc + 1) * 128], eye_sb[:])
                        inc(i, "s_mm", s_mm, 1)
                    tensor.wait_ge(s_sc, strA(t, 1))
                    for j in range(4):
                        if j >= 2:
                            tensor.wait_ge(s_sc, skA(t, j - 2))
                        elif t > 0:
                            tensor.wait_ge(s_sc, svA(t - 1, 2 + (j % 2)))
                        for dc in range(DC):
                            tensor.matmul(ps_rot[j % 2][:], xnT_sb[:, dc * 128 : (dc + 1) * 128],
                                          wk_bf[:, dc, j * 512 : (j + 1) * 512],
                                          start=(dc == 0), stop=False)
                        i = tensor.matmul(ps_rot[j % 2][:], onesP_bf[:],
                                          bk_bf[:, j * 512 : (j + 1) * 512], start=False, stop=True)
                        inc(i, "s_mm", s_mm, 1)
                    for j in range(4):
                        if j >= 2:
                            tensor.wait_ge(s_sc, svA(t, j - 2))
                        else:
                            tensor.wait_ge(s_sc, skA(t, 2 + (j % 2)))
                        for dc in range(DC):
                            tensor.matmul(ps_rot[j % 2][:], xnT_sb[:, dc * 128 : (dc + 1) * 128],
                                          wv_bf[:, dc, j * 512 : (j + 1) * 512],
                                          start=(dc == 0), stop=False)
                        i = tensor.matmul(ps_rot[j % 2][:], onesP_bf[:],
                                          bv_bf[:, j * 512 : (j + 1) * 512], start=False, stop=True)
                        inc(i, "s_mm", s_mm, 1)
                    tensor.wait_ge(s_sc, svA(t, 3))
                    for h in range(H):
                        b, k4 = h // 4, h % 4
                        tensor.matmul(ps_kv[b][:, k4 * 128 : (k4 + 1) * 128],
                                      K_sb[:, h * 128 : (h + 1) * 128],
                                      V_sb[:, h * 128 : (h + 1) * 128],
                                      start=(t == 0), stop=(t == NT - 1), skip_group_check=True)
                    for h in range(H):
                        i = tensor.matmul(ps_ks[:, h : h + 1], K_sb[:, h * 128 : (h + 1) * 128],
                                          onesC_bf[:], start=(t == 0), stop=(t == NT - 1),
                                          skip_group_check=True)
                    inc(i, "s_mm", s_mm, 1)
                bdone(tensor)

            @blk.gpsimd
            def _(gpsimd):
                bwait(gpsimd)
                for k in GA:
                    g_front(gpsimd, k)
                bdone(gpsimd)

        BB[0] += 1
        esA.close()

        arena = es0.enter_context(nc.sbuf_tensor("arena", [128, 76800], BF16))
        xnT_own = arena[:, 0:16384].rearrange("p (a b) -> p a b", a=DC)
        esB1 = contextlib.ExitStack()
        xtb1_0 = esB1.enter_context(nc.sbuf_tensor("xtb1_0", [128, D], F32))
        xtb1 = [xtb1_0, xtb1_0]
        xnb1 = arena[:, 16384:18432]
        ps_trb = esB1.enter_context(nc.psum_tensor("ps_trb", [128, 1024], BF16))
        ps_acc1 = esB1.enter_context(nc.psum_tensor("ps_acc1", [128, 2048], F32))
        PS_ACC[0] = ps_acc1

        VPT1, SPT1, TPT1 = 6, 3, 2
        VB0, SB0, MB0 = C["s_ve"], C["s_sc"], C["s_mm"]
        XB1 = [C["s_x0"], C["s_x1"]]

        def vB1(t, k):
            return VB0 + t * VPT1 + k

        def sB1(t, k):
            return SB0 + t * SPT1 + k

        def mB1(t, k):
            return MB0 + t * TPT1 + k

        with nc.Block() as blk:
            @blk.sync
            def _(sync):
                bwait(sync)
                for t in range(OT):
                    if t >= 1:
                        sync.wait_ge(s_ve, vB1(t - 1, 4))
                    d = sync.dma_start(out=xtb1[0][:], in_=xo[t * 128 : (t + 1) * 128, :])
                    inc(d, "s_x0", s_xp[0], 16)
                bdone(sync)

            @blk.scalar
            def _(scalar):
                bwait(scalar)
                for t in range(OT):
                    scalar.wait_ge(s_ve, vB1(t, 1))
                    i = ln_scalar_pre(scalar, xtb1[0], t % 2)
                    inc(i, "s_sc", s_sc, 1)
                    scalar.wait_ge(s_ve, vB1(t, 2))
                    i = ln_scalar_mid(scalar, t % 2)
                    inc(i, "s_sc", s_sc, 1)
                    scalar.wait_ge(s_ve, vB1(t, 3))
                    i = ln_scalar_post(scalar, t % 2)
                    inc(i, "s_sc", s_sc, 1)
                bdone(scalar)

            @blk.vector
            def _(vector):
                bwait(vector)
                for t in range(OT):
                    vector.wait_ge(s_xp[0], XB1[0] + 16 * (t + 1))
                    i = ln_vec_sum(vector, xtb1[0], t % 2)
                    inc(i, "s_ve", s_ve, 1)
                    vector.wait_ge(s_sc, sB1(t, 1))
                    i = ln_vec_stats(vector, t % 2)
                    inc(i, "s_ve", s_ve, 1)
                    vector.wait_ge(s_sc, sB1(t, 2))
                    i = ln_vec_recip(vector, t % 2)
                    inc(i, "s_ve", s_ve, 1)
                    vector.wait_ge(s_sc, sB1(t, 3))
                    i = ln_vec_xn(vector, xtb1[0], xnb1, t % 2)
                    inc(i, "s_ve", s_ve, 1)
                    for g in range(2):
                        vector.wait_ge(s_mm, mB1(t, 1 + g))
                        dst = xnT_own[:, g * 8 : (g + 1) * 8, t * 128 : (t + 1) * 128]
                        i = vector.tensor_copy(dst, ps_trb[:].rearrange("p (a b) -> p a b", a=8))
                        inc(i, "s_ve", s_ve, 1)
                    for k in g_slice(GB1, t, OT):
                        g_vec_a(vector, k)
                bdone(vector)

            @blk.tensor
            def _(tensor):
                bwait(tensor)
                for t in range(OT):
                    tensor.wait_ge(s_ve, vB1(t, 4))
                    for g in range(2):
                        if g == 1:
                            tensor.wait_ge(s_ve, vB1(t, 5))
                        elif t > 0:
                            tensor.wait_ge(s_ve, vB1(t - 1, 6))
                        for kk in range(8):
                            dc = g * 8 + kk
                            i = tensor.transpose(ps_trb[:, kk * 128 : (kk + 1) * 128],
                                                 xnb1[:, dc * 128 : (dc + 1) * 128], eye_sb[:])
                        inc(i, "s_mm", s_mm, 1)
                    for k in g_slice(GB1, t, OT):
                        g_pe(tensor, k)
                bdone(tensor)

            @blk.gpsimd
            def _(gpsimd):
                bwait(gpsimd)
                for k in GB1:
                    g_front(gpsimd, k)
                bdone(gpsimd)

        BB[0] += 1
        esB1.close()

        Q_sb = arena[:, 16384:32768].rearrange("p (a b) -> p a b", a=H)
        esB2 = contextlib.ExitStack()
        xtb2_0 = esB2.enter_context(nc.sbuf_tensor("xtb2_0", [128, D], F32))
        xtb2 = [xtb2_0, xtb2_0]
        bq_bf = wt2[0:1, 0:D]
        wbig = arena[:, 32768:65536].rearrange("p (a b) -> p a b", a=DC)
        ps_q = [esB2.enter_context(nc.psum_tensor(f"ps_q{i}", [128, 512], F32)) for i in range(4)]
        ps_acc2 = esB2.enter_context(nc.psum_tensor("ps_acc2", [128, 2048], F32))

        VB2, SB2, MB2 = C["s_ve"], C["s_sc"], C["s_mm"]
        XB2 = [C["s_x0"], C["s_x1"]]
        PS_ACC[0] = ps_acc2
        with nc.Block() as blk:
            @blk.sync
            def _(sync):
                bwait(sync)
                d = sync.dma_start(out=xtb2[0][0:1, :], in_=bq)
                inc(d, "s_x0", s_xp[0], 16)
                for dc in range(DC):
                    sync.wait_ge(s_sc, SB2 + dc + 1)
                    d = sync.dma_start(out=xtb2[0][:], in_=wqT[dc * 128 : (dc + 1) * 128, :])
                    inc(d, "s_x0", s_xp[0], 16)
                bdone(sync)

            @blk.scalar
            def _(scalar):
                bwait(scalar)
                scalar.wait_ge(s_xp[0], XB2[0] + 16)
                i = scalar.copy(bq_bf[:], xtb2[0][0:1, :])
                inc(i, "s_sc", s_sc, 1)
                for dc in range(DC):
                    scalar.wait_ge(s_xp[0], XB2[0] + 16 * (dc + 2))
                    i = scalar.copy(wbig[:, dc, :], xtb2[0][:])
                    inc(i, "s_sc", s_sc, 1)
                bdone(scalar)

            @blk.tensor
            def _(tensor):
                bwait(tensor)
                tensor.wait_ge(s_sc, SB2 + DC + 1)
                u = 0
                for qc in range(H):
                    for h2 in range(2):
                        if u >= 4:
                            tensor.wait_ge(s_ve, VB2 + (u - 4) + 1)
                        pq = ps_q[u % 4]
                        for dc in range(DC):
                            tensor.matmul(pq[:], wbig[:, dc, qc * 128 : (qc + 1) * 128],
                                          xnT_own[:, dc, h2 * 512 : (h2 + 1) * 512],
                                          start=(dc == 0), stop=False)
                        i = tensor.matmul(pq[:], bq_bf[:, qc * 128 : (qc + 1) * 128],
                                          ones_bf[:, 0:512], start=False, stop=True)
                        inc(i, "s_mm", s_mm, 1)
                        for k in g_slice(GB2, u, 32):
                            g_pe(tensor, k)
                        u += 1
                bdone(tensor)

            @blk.vector
            def _(vector):
                bwait(vector)
                u = 0
                for qc in range(H):
                    for h2 in range(2):
                        vector.wait_ge(s_mm, MB2 + u + 1)
                        i = vector.tensor_scalar(Q_sb[:, qc, h2 * 512 : (h2 + 1) * 512],
                                                 ps_q[u % 4][:], 0.0, None, ALU.max)
                        inc(i, "s_ve", s_ve, 1)
                        for k in g_slice(GB2, u, 32):
                            g_vec_a(vector, k)
                        u += 1
                bdone(vector)

            @blk.gpsimd
            def _(gpsimd):
                bwait(gpsimd)
                for k in GB2:
                    g_front(gpsimd, k)
                bdone(gpsimd)

        BB[0] += 1
        esB2.close()

        attnOT = arena[:, 32768:49152].rearrange("p (a b) -> p a b", a=DC)
        esB3 = contextlib.ExitStack()
        aO_sb = esB3.enter_context(nc.sbuf_tensor("aO_sb", [128, D], BF16))
        ps_z = esB3.enter_context(nc.psum_tensor("ps_z", [128, H], F32))
        ps_o = [esB3.enter_context(nc.psum_tensor(f"ps_o{i}", [128, 512], F32)) for i in range(2)]
        ps_tr3 = esB3.enter_context(nc.psum_tensor("ps_tr3", [128, 1024], BF16))
        ps_acc3 = esB3.enter_context(nc.psum_tensor("ps_acc3", [128, 2048], F32))
        PS_ACC[0] = ps_acc3

        VB3, SB3, MB3 = C["s_ve"], C["s_sc"], C["s_mm"]
        TPT3, VPT3, SPT3 = 7, 7, 2

        def mB3(t, k):
            return MB3 + t * TPT3 + k

        def vB3(t, k):
            return VB3 + t * VPT3 + k

        def sB3(t, k):
            return SB3 + t * SPT3 + k

        with nc.Block() as blk:
            @blk.tensor
            def _(tensor):
                bwait(tensor)
                for t in range(OT):
                    if t > 0:
                        tensor.wait_ge(s_sc, sB3(t - 1, 1))
                    for h in range(H):
                        i = tensor.matmul(ps_z[:, h : h + 1], Q_sb[:, h, t * 128 : (t + 1) * 128],
                                          Ksum_sb[:, h : h + 1], start=True, stop=True,
                                          skip_group_check=True)
                    inc(i, "s_mm", s_mm, 1)
                    for bb in range(4):
                        if bb >= 2:
                            tensor.wait_ge(s_ve, vB3(t, 2 + (bb - 2)))
                        elif t > 0:
                            tensor.wait_ge(s_ve, vB3(t - 1, 4 + bb))
                        for k4 in range(4):
                            h = bb * 4 + k4
                            i = tensor.matmul(ps_o[bb % 2][:, k4 * 128 : (k4 + 1) * 128],
                                              Q_sb[:, h, t * 128 : (t + 1) * 128],
                                              KV_sb[:, h * 128 : (h + 1) * 128],
                                              start=True, stop=True, skip_group_check=True)
                        inc(i, "s_mm", s_mm, 1)
                    tensor.wait_ge(s_ve, vB3(t, 5))
                    for g in range(2):
                        if g == 1:
                            tensor.wait_ge(s_ve, vB3(t, 6))
                        elif t > 0:
                            tensor.wait_ge(s_ve, vB3(t - 1, 7))
                        for kk in range(8):
                            mc = g * 8 + kk
                            i = tensor.transpose(ps_tr3[:, kk * 128 : (kk + 1) * 128],
                                                 aO_sb[:, mc * 128 : (mc + 1) * 128], eye_sb[:])
                        inc(i, "s_mm", s_mm, 1)
                    for k in g_slice(GB3, t, OT):
                        g_pe(tensor, k)
                bdone(tensor)

            @blk.scalar
            def _(scalar):
                bwait(scalar)
                for t in range(OT):
                    scalar.wait_ge(s_mm, mB3(t, 1))
                    i = scalar.activation(zt_sb[:], ps_z[:], AF.Relu, bias=epsat[:, 0:1],
                                          scale=dq_sb[:, t : t + 1])
                    inc(i, "s_sc", s_sc, 1)
                    scalar.wait_ge(s_ve, vB3(t, 1))
                    i = scalar.mul(zde_sb[:], zr_sb[:], dq_sb[:, t : t + 1])
                    inc(i, "s_sc", s_sc, 1)
                bdone(scalar)

            @blk.vector
            def _(vector):
                bwait(vector)
                for t in range(OT):
                    vector.wait_ge(s_sc, sB3(t, 1))
                    i = vector.reciprocal(zr_sb[:], zt_sb[:])
                    inc(i, "s_ve", s_ve, 1)
                    vector.wait_ge(s_sc, sB3(t, 2))
                    for bb in range(4):
                        vector.wait_ge(s_mm, mB3(t, 2 + bb))
                        for k4 in range(4):
                            h = bb * 4 + k4
                            vector.tensor_scalar(aO_sb[:, h * 128 : (h + 1) * 128],
                                                 ps_o[bb % 2][:, k4 * 128 : (k4 + 1) * 128],
                                                 zde_sb[:, h : h + 1], None, ALU.mult)
                        i = vector.drain()
                        inc(i, "s_ve", s_ve, 1)
                    for g in range(2):
                        vector.wait_ge(s_mm, mB3(t, 6 + g))
                        dst = attnOT[:, g * 8 : (g + 1) * 8, t * 128 : (t + 1) * 128]
                        i = vector.tensor_copy(dst, ps_tr3[:].rearrange("p (a b) -> p a b", a=8))
                        inc(i, "s_ve", s_ve, 1)
                    for k in g_slice(GB3, t, OT):
                        g_vec_a(vector, k)
                bdone(vector)

            @blk.gpsimd
            def _(gpsimd):
                bwait(gpsimd)
                for k in GB3:
                    g_front(gpsimd, k)
                bdone(gpsimd)

            @blk.sync
            def _(sync):
                bwait(sync)
                bdone(sync)

        BB[0] += 1
        esB3.close()

        woacc = arena[:, 49152:65536].rearrange("p (a b) -> p a b", a=OT)
        esB5a = contextlib.ExitStack()
        xtb5_0 = esB5a.enter_context(nc.sbuf_tensor("xtb5_0", [128, 1024], F32))
        xtb5 = [xtb5_0, xtb5_0]
        bo_bf = wt2[0:1, 0:D]
        wohalf = arena[:, 16384:32768].rearrange("p (a b) -> p a b", a=DC)
        ps_w = [esB5a.enter_context(nc.psum_tensor(f"ps_w{i}", [128, 1024], F32)) for i in range(2)]
        ps_acc5 = esB5a.enter_context(nc.psum_tensor("ps_acc5", [128, 2048], F32))
        PS_ACC[0] = ps_acc5

        VB5a, SB5a, MB5a = C["s_ve"], C["s_sc"], C["s_mm"]
        XB5a = [C["s_x0"], C["s_x1"]]
        with nc.Block() as blk:
            @blk.sync
            def _(sync):
                bwait(sync)
                d = sync.dma_start(out=xtb5[0][0:1, :], in_=bo[0:1, 0:1024])
                inc(d, "s_x0", s_xp[0], 16)
                sync.wait_ge(s_sc, SB5a + 1)
                d = sync.dma_start(out=xtb5[0][0:1, :], in_=bo[0:1, 1024:2048])
                inc(d, "s_x0", s_xp[0], 16)
                for hf in range(2):
                    for dc in range(DC):
                        j = hf * DC + dc
                        sync.wait_ge(s_sc, SB5a + j + 2)
                        d = sync.dma_start(out=xtb5[0][:],
                                           in_=woT[dc * 128 : (dc + 1) * 128,
                                                   hf * 1024 : (hf + 1) * 1024])
                        inc(d, "s_x0", s_xp[0], 16)
                bdone(sync)

            @blk.scalar
            def _(scalar):
                bwait(scalar)
                scalar.wait_ge(s_xp[0], XB5a[0] + 16)
                i = scalar.copy(bo_bf[0:1, 0:1024], xtb5[0][0:1, :])
                inc(i, "s_sc", s_sc, 1)
                scalar.wait_ge(s_xp[0], XB5a[0] + 32)
                i = scalar.copy(bo_bf[0:1, 1024:2048], xtb5[0][0:1, :])
                inc(i, "s_sc", s_sc, 1)
                for hf in range(2):
                    for dc in range(DC):
                        j = hf * DC + dc
                        scalar.wait_ge(s_xp[0], XB5a[0] + 16 * (j + 3))
                        if hf == 1:
                            scalar.wait_ge(s_mm, MB5a + 8)
                        i = scalar.copy(wohalf[:, dc, :], xtb5[0][:])
                        inc(i, "s_sc", s_sc, 1)
                bdone(scalar)

            @blk.tensor
            def _(tensor):
                bwait(tensor)
                u = 0
                for hf in range(2):
                    tensor.wait_ge(s_sc, SB5a + DC * (hf + 1) + 2)
                    for t in range(OT):
                        if u >= 2:
                            tensor.wait_ge(s_ve, VB5a + (u - 2) + 1)
                        pw = ps_w[u % 2]
                        for mc in range(DC):
                            for j2 in range(2):
                                tensor.matmul(pw[:, j2 * 512 : (j2 + 1) * 512],
                                              attnOT[:, mc, t * 128 : (t + 1) * 128],
                                              wohalf[:, mc, j2 * 512 : (j2 + 1) * 512],
                                              start=(mc == 0), stop=False)
                        for j2 in range(2):
                            i = tensor.matmul(pw[:, j2 * 512 : (j2 + 1) * 512], onesP_bf[:],
                                              bo_bf[:, hf * 1024 + j2 * 512 : hf * 1024 + (j2 + 1) * 512],
                                              start=False, stop=True)
                        inc(i, "s_mm", s_mm, 1)
                        for k in g_slice(GB5a, u, 16):
                            g_pe(tensor, k)
                        u += 1
                bdone(tensor)

            @blk.vector
            def _(vector):
                bwait(vector)
                u = 0
                for hf in range(2):
                    for t in range(OT):
                        for k in g_slice(GB5a, u, 16):
                            g_vec_a(vector, k)
                        vector.wait_ge(s_mm, MB5a + u + 1)
                        i = vector.tensor_copy(woacc[:, t, hf * 1024 : (hf + 1) * 1024], ps_w[u % 2][:])
                        inc(i, "s_ve", s_ve, 1)
                        u += 1
                bdone(vector)

            @blk.gpsimd
            def _(gpsimd):
                bwait(gpsimd)
                for k in GB5a:
                    g_front(gpsimd, k)
                bdone(gpsimd)

        BB[0] += 1
        esB5a.close()

        hT_a = arena[:, 16384:49152].rearrange("p (a b) -> p a b", a=32)
        hT_b = arena[:, 65536:76800].rearrange("p (a b) -> p a b", a=IC - 32)

        def hT_sl(ic, fsl):
            return hT_a[:, ic, fsl] if ic < 32 else hT_b[:, ic - 32, fsl]
        esB4 = contextlib.ExitStack()
        w1sb_ = esB4.enter_context(nc.sbuf_tensor("w1sb", [128, D], BF16))
        w2sb_ = esB4.enter_context(nc.sbuf_tensor("w2sb", [128, D], BF16))
        w1b = [w1sb_, KV_sb]
        w2b = [w2sb_, wt2]
        silu1 = esB4.enter_context(nc.sbuf_tensor("silu0", [128, 512], F32))
        silu_sb = [silu1, silu1]
        ps_g = [esB4.enter_context(nc.psum_tensor(f"ps_g{i}", [128, 1024], F32)) for i in range(2)]
        ps_u = [esB4.enter_context(nc.psum_tensor(f"ps_u{i}", [128, 1024], F32)) for i in range(2)]

        VB4, SB4, MB4 = C["s_ve"], C["s_sc"], C["s_mm"]
        XB4 = [C["s_x0"], C["s_x1"]]
        with nc.Block() as blk:
            @blk.sync
            def _(sync):
                bwait(sync)
                for ic in range(IC):
                    sync.wait_ge(s_go, 16 * (2 * ic + 2))
                    if ic >= 2:
                        sync.wait_ge(s_mm, MB4 + 4 * (ic - 1))
                    if ic >= 1:
                        sync.wait_ge(s_xp[0], XB4[0] + 16 * ic)
                        sync.wait_ge(s_xp[1], XB4[1] + 16 * ic)
                    d = sync.dma_start(out=w1b[ic % 2][:], in_=wh[0][ic, :, :])
                    inc(d, "s_x0", s_xp[0], 16)
                    d = sync.dma_start(out=w2b[ic % 2][:], in_=wh[1][ic, :, :])
                    inc(d, "s_x1", s_xp[1], 16)
                bdone(sync)

            @blk.tensor
            def _(tensor):
                bwait(tensor)
                for ic in range(IC):
                    tensor.wait_ge(s_xp[0], XB4[0] + 16 * (ic + 1))
                    tensor.wait_ge(s_xp[1], XB4[1] + 16 * (ic + 1))
                    for h2 in range(2):
                        if ic >= 2:
                            tensor.wait_ge(s_ve, VB4 + 2 * (ic - 2) + h2 + 1)
                        pg, pu = ps_g[ic % 2], ps_u[ic % 2]
                        for dc in range(DC):
                            i = tensor.matmul(pg[:, h2 * 512 : (h2 + 1) * 512],
                                              w1b[ic % 2][:, dc * 128 : (dc + 1) * 128],
                                              xnT_own[:, dc, h2 * 512 : (h2 + 1) * 512],
                                              start=(dc == 0), stop=(dc == DC - 1))
                        inc(i, "s_mm", s_mm, 1)
                        for dc in range(DC):
                            i = tensor.matmul(pu[:, h2 * 512 : (h2 + 1) * 512],
                                              w2b[ic % 2][:, dc * 128 : (dc + 1) * 128],
                                              xnT_own[:, dc, h2 * 512 : (h2 + 1) * 512],
                                              start=(dc == 0), stop=(dc == DC - 1))
                        inc(i, "s_mm", s_mm, 1)
                bdone(tensor)

            @blk.scalar
            def _(scalar):
                bwait(scalar)
                for ic in range(IC):
                    for h2 in range(2):
                        scalar.wait_ge(s_mm, MB4 + 4 * ic + 2 * h2 + 1)
                        if ic > 0 or h2 > 0:
                            scalar.wait_ge(s_ve, VB4 + 2 * ic + h2)
                        i = scalar.activation(silu_sb[h2][:], ps_g[ic % 2][:, h2 * 512 : (h2 + 1) * 512],
                                              AF.Silu)
                        inc(i, "s_sc", s_sc, 1)
                bdone(scalar)

            @blk.vector
            def _(vector):
                bwait(vector)
                for ic in range(IC):
                    for h2 in range(2):
                        vector.wait_ge(s_sc, SB4 + 2 * ic + h2 + 1)
                        vector.wait_ge(s_mm, MB4 + 4 * ic + 2 * h2 + 2)
                        i = vector.tensor_tensor(hT_sl(ic, slice(h2 * 512, (h2 + 1) * 512)),
                                                 silu_sb[h2][:],
                                                 ps_u[ic % 2][:, h2 * 512 : (h2 + 1) * 512],
                                                 ALU.mult)
                        inc(i, "s_ve", s_ve, 1)
                    for k in g_slice(GB4, ic, IC):
                        g_vec(vector, k)
                bdone(vector)

            @blk.gpsimd
            def _(gpsimd):
                bwait(gpsimd)
                for k in GB4:
                    g_front(gpsimd, k)
                bdone(gpsimd)

        BB[0] += 1
        esB4.close()

        esB5b = contextlib.ExitStack()
        NW3 = 3
        w3qb = [arena[:, i * 512 : (i + 1) * 512] for i in range(NW3)]
        xo8 = [esB5b.enter_context(nc.sbuf_tensor(f"xo8_{i}", [128, 512], F32)) for i in range(4)]
        ps8 = [esB5b.enter_context(nc.psum_tensor(f"ps8_{i}", [128, 512], F32)) for i in range(8)]

        VB5, SB5, MB5 = C["s_ve"], C["s_sc"], C["s_mm"]
        XB5, XO0 = C["s_x0"], C["s_xo"]

        def fB5(j):
            return MB5 + (j // IC) * (IC + 1) + 2 + (j % IC)

        with nc.Block() as blk:
            @blk.scalar
            def _(scalar):
                bwait(scalar)
                bdone(scalar)

            @blk.sync
            def _(sync):
                bwait(sync)
                for q in range(4):
                    for ic in range(IC):
                        j = IC * q + ic
                        sync.wait_ge(s_go, 16 * (2 * IC + ic + 1))
                        if j >= NW3:
                            sync.wait_ge(s_mm, fB5(j - NW3))
                        if j >= 1:
                            sync.wait_ge(s_xp[0], XB5 + 16 * j)
                        d = sync.dma_start(out=w3qb[j % NW3],
                                           in_=wh[2][ic, :, q * 512 : (q + 1) * 512])
                        inc(d, "s_x0", s_xp[0], 16)
                    for t in range(8):
                        m = 8 * q + t
                        if m >= 4:
                            n = (m - 4 - t % 2) // 2 + 1
                            sync.wait_ge(s_oa[t % 2], 16 * n)
                        if m >= 1:
                            sync.wait_ge(s_xo, XO0 + 16 * m)
                        d = sync.dma_start(out=xo8[t % 4][:],
                                           in_=xo[t * 128 : (t + 1) * 128, q * 512 : (q + 1) * 512])
                        inc(d, "s_xo", s_xo, 16)
                bdone(sync)

            @blk.tensor
            def _(tensor):
                bwait(tensor)
                for q in range(4):
                    if q > 0:
                        tensor.wait_ge(s_ve, VB5 + 8 * q)
                    for t in range(8):
                        i = tensor.matmul(ps8[t][:], eye_sb[:],
                                          woacc[:, t, q * 512 : (q + 1) * 512],
                                          start=True, stop=False, skip_group_check=True)
                    inc(i, "s_mm", s_mm, 1)
                    for ic in range(IC):
                        j = IC * q + ic
                        tensor.wait_ge(s_xp[0], XB5 + 16 * (j + 1))
                        for t in range(8):
                            i = tensor.matmul(ps8[t][:],
                                              hT_sl(ic, slice(t * 128, (t + 1) * 128)),
                                              w3qb[j % NW3], start=False, stop=(ic == IC - 1),
                                              skip_group_check=True)
                        inc(i, "s_mm", s_mm, 1)
                bdone(tensor)

            @blk.vector
            def _(vector):
                bwait(vector)
                for q in range(4):
                    vector.wait_ge(s_mm, MB5 + (IC + 1) * (q + 1))
                    for t in range(8):
                        vector.wait_ge(s_xo, XO0 + 16 * (8 * q + t + 1))
                        i = vector.tensor_tensor(xo8[t % 4][:], ps8[t][:], xo8[t % 4][:], ALU.add)
                        inc(i, "s_ve", s_ve, 1)
                bdone(vector)

            @blk.gpsimd
            def _(gpsimd):
                bwait(gpsimd)
                for q in range(4):
                    for t in range(8):
                        gpsimd.wait_ge(s_ve, VB5 + 8 * q + t + 1)
                        n = (8 * q + t - t % 2) // 2
                        if n > 0:
                            gpsimd.wait_ge(s_oa[t % 2], 16 * n)
                        d = gpsimd.dma_start(out=y[t * 128 : (t + 1) * 128, q * 512 : (q + 1) * 512],
                                             in_=xo8[t % 4][:])
                        inc(d, f"s_out{t % 2}", s_oa[t % 2], 16)
                gpsimd.wait_ge(s_oa[0], 16 * 16)
                gpsimd.wait_ge(s_oa[1], 16 * 16)
                bdone(gpsimd)

        BB[0] += 1
        esB5b.close()

    return nc


_NC_CACHE = [None]


def kernel(L0_attn_x, L0_attn_Wq, L0_attn_bq, L0_attn_Wk, L0_attn_bk,
           L0_attn_Wv, L0_attn_bv, L0_attn_Wo, L0_attn_bo,
           L0_ffn_lut, L0_ffn_gate_idx, L0_ffn_up_idx, L0_ffn_down_idx):
    x = np.ascontiguousarray(np.asarray(L0_attn_x, np.float32))

    def f32(a):
        return np.ascontiguousarray(np.asarray(a, np.float32))

    wqT = f32(np.asarray(L0_attn_Wq).T)
    wkT = f32(np.asarray(L0_attn_Wk).T)
    wvT = f32(np.asarray(L0_attn_Wv).T)
    woT = f32(np.asarray(L0_attn_Wo).T)
    bq_ = f32(L0_attn_bq).reshape(1, D)
    bk_ = f32(L0_attn_bk).reshape(1, D)
    bv_ = f32(L0_attn_bv).reshape(1, D)
    bo_ = f32(L0_attn_bo).reshape(1, D)
    lut = f32(L0_ffn_lut).reshape(1, 16)

    gidx = np.asarray(L0_ffn_gate_idx, np.int32)
    uidx = np.asarray(L0_ffn_up_idx, np.int32)
    didx = np.asarray(L0_ffn_down_idx, np.int32)
    g1 = np.ascontiguousarray(
        gidx.astype(np.int8).reshape(IC, 128, DC, 128).transpose(0, 3, 2, 1)).reshape(IC, 128, D)
    g2 = np.ascontiguousarray(
        uidx.astype(np.int8).reshape(IC, 128, DC, 128).transpose(0, 3, 2, 1)).reshape(IC, 128, D)
    d3 = np.ascontiguousarray(didx.astype(np.int8).T).reshape(IC, 128, D)

    pos = np.arange(S, dtype=np.float64)
    dec = np.exp(-pos / S).astype(np.float32)
    dk_cols = np.ascontiguousarray(dec.reshape(NT, 128).T)
    eye = np.eye(128).astype(ml_dtypes.bfloat16)

    if _NC_CACHE[0] is None:
        _NC_CACHE[0] = build()
    nc = _NC_CACHE[0]

    in_maps = []
    for c in range(8):
        b, q = c // 4, c % 4
        in_maps.append({
            "xb": np.ascontiguousarray(x[b]),
            "xo": np.ascontiguousarray(x[b, q * T : (q + 1) * T]),
            "wkT": wkT, "wvT": wvT, "wqT": wqT, "woT": woT,
            "bk": bk_, "bv": bv_, "bq": bq_, "bo": bo_,
            "g1": g1, "g2": g2, "d3": d3,
            "lut16": lut,
            "dk_cols": dk_cols,
            "dq_cols": np.ascontiguousarray(dk_cols[:, q * OT : (q + 1) * OT]),
            "eye_in": eye,
        })
    res = run_bass_kernel_spmd(nc, in_maps, core_ids=list(range(8)))
    out = np.empty((2, S, D), np.float32)
    for c in range(8):
        b, q = c // 4, c % 4
        out[b, q * T : (q + 1) * T] = np.asarray(res.results[c]["y"], np.float32)
    return out
